# version 24
# baseline (speedup 1.0000x reference)
import numpy as np
import concourse.bass as bass
import concourse.mybir as mybir
from concourse.bass_utils import run_bass_kernel_spmd

F32 = mybir.dt.float32
BF16 = mybir.dt.bfloat16
I32 = mybir.dt.int32
AL = mybir.AluOpType
AF = mybir.ActivationFunctionType
AX = mybir.AxisListType.X

D = 1024
TC = 2048
NSUB = 16
NQT = 8
QTT = 256
NH = 16
HD = 64
NPAIR = 8
II = 2816
WSH = II // 8
DSH = D // 8
SHE = 1024 * WSH
GINE = 3 * SHE
STRIPES = [(0, 768), (768, 768), (1536, 640), (2176, 640)]


def head_off(h):
    return 512 * (h // 7) + 65 * (h % 7)


def pair_off(p):
    return 512 * (p // 3) + 130 * (p % 3)


HEAD_GROUPS = [(0, 7), (7, 7), (14, 2)]

MAGIC_H = float(np.float32(1.5 * 2**22))
ROUND_BIT = 0x00200000
MANT_MASK = -0x00400000

ENGS = ["sync", "gpsimd", "vector", "scalar", "tensor"]
DMA_STREAMS = ["dck", "dst_xn", "dst_xq", "dst_k", "dst_v", "dgw", "dcg",
               "dmask", "dqx", "dacc", "dkq", "dvq", "dxqt", "dstrip",
               "dout", "dwb", "dgp"]
SEMS = ENGS + ["cc0", "cc1"] + DMA_STREAMS


class P:

    def __init__(self):
        self.items = {e: [] for e in ENGS}
        self.cnt = {e: 0 for e in ENGS}
        self.opidx = {e: 0 for e in ENGS}
        self.drained = {e: 0 for e in ENGS}
        self.waited = {e: {f: 0 for f in SEMS} for e in ENGS}
        self.scnt = {s: 0 for s in SEMS}

    def dep(self, e, d):
        if d is None:
            return
        de, didx, dval, ddma = d
        if de not in ENGS:
            if dval > self.waited[e][de]:
                self.items[e].append(("wait", (de, dval), []))
                self.waited[e][de] = dval
            return
        if de == e and not ddma:
            if didx > self.drained[e]:
                self.items[e].append(("drain", None, []))
                self.drained[e] = self.opidx[e]
        else:
            if dval > self.waited[e][de]:
                self.items[e].append(("wait", (de, dval), []))
                self.waited[e][de] = dval

    def op(self, e, fn, deps=(), dma=False, inc=True):
        best = {}
        need_drain = False
        drain_idx = 0
        for d in deps:
            if d is None:
                continue
            de, didx, dval, ddma = d
            if de == e and ddma is False and de in ENGS:
                need_drain = True
                drain_idx = max(drain_idx, didx)
            else:
                if dval > best.get(de, 0):
                    best[de] = dval
        for de, dval in best.items():
            self.dep(e, (de, 0, dval, True))
        if need_drain:
            self.dep(e, (e, drain_idx, 0, False))
        self.opidx[e] += 1
        idx = self.opidx[e]
        incs = []
        if isinstance(dma, str):
            self.scnt[dma] += 16
            incs = [(dma, 16)]
            self.items[e].append(("op", fn, incs))
            return (dma, idx, self.scnt[dma], True)
        if inc or dma:
            n = 16 if dma else 1
            self.cnt[e] += n
            incs = [(e, n)]
        self.items[e].append(("op", fn, incs))
        return (e, idx, self.cnt[e], dma)

    def barrier_tag(self, e):
        return (e, self.opidx[e], self.cnt[e], False)

    def stream_tag(self, s):
        return (s, 0, self.scnt[s], True)

    def emit(self, block, sems):
        def run(e, eng):
            for kind, a, incs in self.items[e]:
                if kind == "drain":
                    eng.drain()
                elif kind == "wait":
                    eng.wait_ge(sems[a[0]], a[1])
                else:
                    ins = a(eng)
                    for (se, n) in incs:
                        ins.then_inc(sems[se], n)

        block.sync(lambda eng: run("sync", eng))
        block.gpsimd(lambda eng: run("gpsimd", eng))
        block.vector(lambda eng: run("vector", eng))
        block.scalar(lambda eng: run("scalar", eng))
        block.tensor(lambda eng: run("tensor", eng))


def emit_quant(p, eng, x_ap, q_ap, bm, sc, rc, t1, t2, t3, F, dep,
               final_war=(), mbc=None):
    NB = F // 16
    xv = lambda ap: ap.rearrange("p (n b) -> p n b", b=16)
    V, G = "vector", "gpsimd"
    dep = list(dep)
    mbcb = mbc.unsqueeze(2).broadcast_to([128, 1, F]).squeeze(1)
    tr = p.op(V, lambda E: E.reduce_max(bm, xv(x_ap), axis=AX,
                                        apply_absolute_value=True), deps=dep)
    ts = p.op(G, lambda E: E.tensor_scalar(sc, bm, 6e-12, float(1.0 / 6.0),
                                           AL.max, AL.mult), deps=[tr])
    tc = p.op(V, lambda E: E.reciprocal(rc, sc), deps=[ts])
    rcb = rc.unsqueeze(2).broadcast_to([128, NB, 16])
    scb = sc.unsqueeze(2).broadcast_to([128, NB, 16])
    tb = p.op(G, lambda E: E.tensor_tensor(xv(t2), xv(x_ap), rcb, AL.mult),
              deps=[tc] + dep)
    p.q_rb_tag = tb
    rb = t2
    tk = p.op(V, lambda E: E.tensor_scalar(t1, rb, MAGIC_H, MAGIC_H,
                                           AL.add, AL.subtract), deps=[tb])
    tp_ = p.op(V, lambda E: E.tensor_scalar(t3.bitcast(I32), rb.bitcast(I32),
                                            ROUND_BIT, None, AL.add),
               deps=[tb])
    tq1 = p.op(V, lambda E: E.tensor_scalar(t3.bitcast(I32), t3.bitcast(I32),
                                            MANT_MASK, None, AL.bitwise_and),
               deps=[tp_])
    tzc = p.op(V, lambda E: E.tensor_scalar(t2, t3, -1.0, 1.0, AL.max, AL.min),
               deps=[tq1, tk, tp_])
    th_ = p.op(V, lambda E: E.tensor_scalar(t1, t1, -1.0, 1.0, AL.max, AL.min),
               deps=[tk])
    tm = p.op(G, lambda E: E.tensor_tensor(t2, t1, t2, AL.subtract),
              deps=[th_, tzc])
    tl = p.op(V, lambda E: E.tensor_tensor(t3, t2, t3, AL.add),
              deps=[tm, tq1])
    tf = p.op(G, lambda E: E.tensor_tensor(xv(q_ap), xv(t3), scb, AL.mult),
              deps=[tl] + list(final_war))
    return tf


def emit_quant_multi(p, jobs, mbc):
    V, G = "vector", "gpsimd"
    n = len(jobs)
    st = [dict() for _ in range(n)]
    xvs = []
    for j in jobs:
        F = j["F"]
        j["NB"] = F // 16
        xvs.append(None)

    def xv(ap):
        return ap.rearrange("p (n b) -> p n b", b=16)

    for idx, j in enumerate(jobs):
        st[idx]["r"] = p.op(V, (lambda j: lambda E: E.reduce_max(
            j["bm"], xv(j["x"]), axis=AX, apply_absolute_value=True))(j),
            deps=list(j["dep"]))
    for idx, j in enumerate(jobs):
        st[idx]["s"] = p.op(G, (lambda j: lambda E: E.tensor_scalar(
            j["sc"], j["bm"], 6e-12, float(1.0 / 6.0), AL.max, AL.mult))(j),
            deps=[st[idx]["r"]])
    for idx, j in enumerate(jobs):
        st[idx]["c"] = p.op(V, (lambda j: lambda E: E.reciprocal(
            j["rc"], j["sc"]))(j), deps=[st[idx]["s"]])
    for idx, j in enumerate(jobs):
        NB = j["NB"]
        rcb = j["rc"].unsqueeze(2).broadcast_to([128, NB, 16])
        st[idx]["b"] = p.op(G, (lambda j, rcb: lambda E: E.tensor_tensor(
            xv(j["t2"]), xv(j["x"]), rcb, AL.mult))(j, rcb),
            deps=[st[idx]["c"]] + list(j["dep"]))
        st[idx]["b_tag"] = st[idx]["b"]
    for idx, j in enumerate(jobs):
        st[idx]["k"] = p.op(V, (lambda j: lambda E: E.tensor_scalar(
            j["t1"], j["t2"], MAGIC_H, MAGIC_H, AL.add, AL.subtract))(j),
            deps=[st[idx]["b"]])
    for idx, j in enumerate(jobs):
        st[idx]["p"] = p.op(V, (lambda j: lambda E: E.tensor_scalar(
            j["t3"].bitcast(I32), j["t2"].bitcast(I32), ROUND_BIT, None,
            AL.add))(j), deps=[st[idx]["b"]])
    for idx, j in enumerate(jobs):
        st[idx]["q1"] = p.op(V, (lambda j: lambda E: E.tensor_scalar(
            j["t3"].bitcast(I32), j["t3"].bitcast(I32), MANT_MASK, None,
            AL.bitwise_and))(j), deps=[st[idx]["p"]])
    for idx, j in enumerate(jobs):
        st[idx]["zc"] = p.op(V, (lambda j: lambda E: E.tensor_scalar(
            j["t2"], j["t3"], -1.0, 1.0, AL.max, AL.min))(j),
            deps=[st[idx]["q1"], st[idx]["k"]])
    for idx, j in enumerate(jobs):
        st[idx]["hcl"] = p.op(V, (lambda j: lambda E: E.tensor_scalar(
            j["t1"], j["t1"], -1.0, 1.0, AL.max, AL.min))(j),
            deps=[st[idx]["k"]])
    for idx, j in enumerate(jobs):
        st[idx]["m"] = p.op(G, (lambda j: lambda E: E.tensor_tensor(
            j["t2"], j["t1"], j["t2"], AL.subtract))(j),
            deps=[st[idx]["hcl"], st[idx]["zc"]])
    for idx, j in enumerate(jobs):
        st[idx]["l"] = p.op(V, (lambda j: lambda E: E.tensor_tensor(
            j["t3"], j["t2"], j["t3"], AL.add))(j),
            deps=[st[idx]["m"], st[idx]["q1"]])
    outs = []
    for idx, j in enumerate(jobs):
        NB = j["NB"]
        scb = j["sc"].unsqueeze(2).broadcast_to([128, NB, 16])
        outs.append(p.op(G, (lambda j, scb: lambda E: E.tensor_tensor(
            xv(j["q"]), xv(j["t3"]), scb, AL.mult))(j, scb),
            deps=[st[idx]["l"]] + list(j["final_war"])))
    p.q_rb_tags = [st[i]["b_tag"] for i in range(n)]
    return outs


def build_core(stop_after=None, dbg=False):
    nc = bass.Bass("TRN2", target_bir_lowering=False)
    dkind = "ExternalOutput" if dbg else "Internal"

    x_in = nc.dram_tensor("x_c", [TC, D], F32, kind="ExternalInput").ap()
    wqkvt = nc.dram_tensor("wqkvt", [D, 3 * D], F32, kind="ExternalInput").ap()
    woutt = nc.dram_tensor("woutt", [D, D], F32, kind="ExternalInput").ap()
    wg_sh = nc.dram_tensor("wg_sh", [WSH, D], F32, kind="ExternalInput").ap()
    wu_sh = nc.dram_tensor("wu_sh", [WSH, D], F32, kind="ExternalInput").ap()
    wd_sh = nc.dram_tensor("wd_sh", [DSH, II], F32, kind="ExternalInput").ap()
    cmask = nc.dram_tensor("cmask", [8], F32, kind="ExternalInput").ap()
    crank = nc.dram_tensor("crank", [1, 1], I32, kind="ExternalInput").ap()
    out = nc.dram_tensor("out_c", [TC, D], F32, kind="ExternalOutput").ap()

    xnfm_d = nc.dram_tensor("xnfm_d", [D, TC], BF16, kind=dkind).ap()
    xqfm_d = nc.dram_tensor("xqfm_d", [D, TC], BF16, kind=dkind).ap()
    k_d = nc.dram_tensor("k_d", [TC, D], BF16, kind=dkind).ap()
    v_d = nc.dram_tensor("v_d", [TC, NH * 65], BF16, kind=dkind).ap()
    wgin = nc.dram_tensor("wgin", [GINE], BF16)
    wgall = nc.dram_tensor("wgall", [8 * GINE], BF16, addr_space="Shared")
    cgin = nc.dram_tensor("cgin", [128, 1536], F32)
    cgall = nc.dram_tensor("cgall", [8 * 128, 1536], F32, addr_space="Shared")
    if dbg:
        dbg_wg = nc.dram_tensor("dbg_wg", [8 * GINE], BF16,
                                kind="ExternalOutput")
        dbg_cg = nc.dram_tensor("dbg_cg", [8 * 128, 1536], F32,
                                kind="ExternalOutput")
        dbg_cst = nc.dram_tensor("dbg_cst", [128, 1536], F32,
                                 kind="ExternalOutput")
        dbg_mk = nc.dram_tensor("dbg_mk", [128, 8], F32,
                                kind="ExternalOutput")

    from contextlib import ExitStack
    ctx = ExitStack()
    e = ctx.enter_context

    acc = e(nc.sbuf_tensor("acc", [128, NSUB * D], F32))
    cstA = e(nc.sbuf_tensor("cstA", [128, 1536], F32))
    cstB = e(nc.sbuf_tensor("cstB", [128, 1536], F32))
    ident = e(nc.sbuf_tensor("ident", [128, 128], BF16))
    mko = e(nc.sbuf_tensor("mko", [128, 8], F32))

    arf = e(nc.sbuf_tensor("arf", [128, 6144], F32))
    arb = e(nc.sbuf_tensor("arb", [128, 3072], BF16))
    tev = e(nc.sbuf_tensor("tev", [128, 1024], BF16))
    wbig = e(nc.sbuf_tensor("wbig", [128, 8 * 2048], BF16))
    wgs = e(nc.sbuf_tensor("wgs", [128, 8 * 768], BF16))
    wus = e(nc.sbuf_tensor("wus", [128, 8 * 768], BF16))
    wds = e(nc.sbuf_tensor("wds", [128, 6 * 1024], BF16))

    qbm = e(nc.sbuf_tensor("qbm", [128, 64], F32))
    qsc = e(nc.sbuf_tensor("qsc", [128, 64], F32))
    qrc = e(nc.sbuf_tensor("qrc", [128, 64], F32))

    ks = e(nc.sbuf_tensor("ks", [128, D], BF16))
    vs = e(nc.sbuf_tensor("vs", [128, NH * 65], BF16))
    st1 = e(nc.sbuf_tensor("st1", [128, 1], F32))
    st2 = e(nc.sbuf_tensor("st2", [128, 1], F32))
    st3 = e(nc.sbuf_tensor("st3", [128, 1], F32))
    st4 = e(nc.sbuf_tensor("st4", [128, 1], F32))
    st1b = e(nc.sbuf_tensor("st1b", [128, 1], F32))
    st2b = e(nc.sbuf_tensor("st2b", [128, 1], F32))
    st3b = e(nc.sbuf_tensor("st3b", [128, 1], F32))
    st4b = e(nc.sbuf_tensor("st4b", [128, 1], F32))
    qbm2 = e(nc.sbuf_tensor("qbm2", [128, 64], F32))
    qsc2 = e(nc.sbuf_tensor("qsc2", [128, 64], F32))
    qrc2 = e(nc.sbuf_tensor("qrc2", [128, 64], F32))
    epsv = e(nc.sbuf_tensor("epsv", [128, 1], F32))
    mskt = e(nc.sbuf_tensor("mskt", [128, 1], I32))
    crk = e(nc.sbuf_tensor("crk", [1, 1], I32))
    rg_base = nc.sync.alloc_register("rgbase")
    rg_tmp = nc.sync.alloc_register("rgtmp")

    xnq = e(nc.sbuf_tensor("xnq", [128, 8 * QTT], BF16))
    uq = e(nc.sbuf_tensor("uq", [128, 8 * QTT], F32))
    kq = e(nc.sbuf_tensor("kq", [128, D], BF16))
    vq = e(nc.sbuf_tensor("vq", [128, NH * 65], BF16))
    atm = e(nc.sbuf_tensor("atm", [128, D], BF16))
    afm = e(nc.sbuf_tensor("afm", [128, 8 * 128], BF16))
    dq = e(nc.sbuf_tensor("dq", [64, 16], F32))
    dn = e(nc.sbuf_tensor("dn", [64, 16], F32))

    xqt = e(nc.sbuf_tensor("xqt", [128, 8 * 128], BF16))
    hfm = e(nc.sbuf_tensor("hfm", [128, 6 * 128], BF16))

    iwork = arf[:, 0:128].bitcast(I32)
    wbuf = arf[:, 0:3072]
    qslc = [arf[:, 3072:4096], arf[:, 4096:5120], arf[:, 5120:6144]]
    xs = arf[:, 0:1024]
    xn = arf[:, 1024:2048]
    hbuf = arf[:, 0:768]
    fslc = [arf[:, 3072 + 1024 * i: 3072 + 1024 * i + 768] for i in range(3)]
    wq_b = arb[:, 0:3072]
    xnb = arb[:, 0:1024]
    xqb = arb[:, 1024:2048]
    gsil = arb[:, 0:768]
    hqb = arb[:, 1024:1792]
    wgs_f = wgs.ap().bitcast(F32)
    wus_f = wus.ap().bitcast(F32)
    xs1 = wgs_f[:, 0:1024]
    xn1 = wgs_f[:, 1024:2048]
    qA1 = wgs_f[:, 2048:3072]
    qB1 = wus_f[:, 0:1024]
    qC1 = wus_f[:, 1024:2048]
    wbig_f = wbig.ap().bitcast(F32)
    hbuf1 = wbig_f[:, 0:768]
    f1 = [wbig_f[:, 768:1536], wbig_f[:, 1536:2304], wbig_f[:, 2304:3072]]
    qbm3 = wbig_f[:, 3072:3120]
    qsc3 = wbig_f[:, 3120:3168]
    qrc3 = wbig_f[:, 3168:3216]
    wbig_b = wbig[:, :]
    gsil1 = wbig_b[:, 6528:7296]
    hqb1 = wbig_b[:, 7296:8064]
    hfm1 = wbig_b[:, 8064:8832]
    xqt1 = wbig_b[:, 8832:9856]
    xnb1 = wds[:, 0:1024]
    xqb1 = wds[:, 1024:2048]
    tev1b = wds[:, 2048:3072]
    ks1 = wds[:, 3072:4096]
    vs1 = wds[:, 4096:5136]

    pA = e(nc.psum_tensor("pA", [128, 1024], F32))
    pC = e(nc.psum_tensor("pC", [128, 1536], F32))
    pD = e(nc.psum_tensor("pD", [128, 1536], F32))
    pAbf = pA.ap().bitcast(BF16)
    pCbf = pC.ap().bitcast(BF16)
    pDbf = pD.ap().bitcast(BF16)

    sems = {}
    for name in SEMS:
        sems[name] = e(nc.semaphore("s_" + name))
    block = e(nc.Block())

    p = P()

    t_id0 = p.op("gpsimd", lambda E: E.iota(iwork, pattern=[[1, 128]],
                                            base=0, channel_multiplier=-1))
    t_id1 = p.op("gpsimd", lambda E: E.tensor_scalar(iwork, iwork,
                                                     0, None, AL.is_equal),
                 deps=[t_id0])
    t_ident = p.op("gpsimd", lambda E: E.tensor_copy(ident[:, :], iwork),
                   deps=[t_id1])
    t_mask = p.op("sync", lambda E: E.dma_start(
        out=mko[:, :], in_=cmask.unsqueeze(0).broadcast_to([128, 8])),
        dma="dmask")
    t_vones = p.op("vector", lambda E: E.memset(
        vs[:, :].rearrange("p (h c) -> p h c", c=65)[:, :, 64:65], 1.0))
    t_eps = p.op("vector", lambda E: E.memset(epsv[:, :], 1e-5))
    t_msk = p.op("vector", lambda E: E.memset(mskt[:, :], MANT_MASK))
    t_crk = p.op("sync", lambda E: E.dma_start(out=crk[:, :], in_=crank),
                 dma="dmask")
    def _ldrank(E):
        ins = E.reg_load(rg_base, crk[0:1, 0:1])
        E.reg_mul(rg_base, rg_base, GINE)
        return ins
    p.op("sync", _ldrank, deps=[t_crk], inc=False)
    t_pzA = p.op("vector", lambda E: E.memset(pA.ap(), 0.0))
    t_pzC = p.op("vector", lambda E: E.memset(pC.ap(), 0.0))
    t_pzD = p.op("vector", lambda E: E.memset(pD.ap(), 0.0))

    wshard_specs = [
        (wg_sh, [128, 128, 96], D, 0, WSH),
        (wu_sh, [128, 128, 96], D, SHE, WSH),
        (wd_sh, [128], II, 2 * SHE, DSH),
    ]
    t_wgin_writes = []
    prev_tq = None
    prev_last_mm = None
    w_last_tv = None
    for din, rowtiles, rowlen, gbase, grs in wshard_specs:
        n_rt = len(rowtiles)
        ld_tags = []
        for rt, nr in enumerate(rowtiles):
            def mkld(rt=rt, nr=nr, din=din, rowlen=rowlen):
                return lambda E: E.dma_start(
                    out=wbuf[0:nr, rt * rowlen:(rt + 1) * rowlen]
                    if rowlen == D else wbuf[0:nr, 0:rowlen],
                    in_=din[rt * 128: rt * 128 + nr, :])
            ld_tags.append(p.op("sync", mkld(),
                                deps=([prev_tq] if rt == 0 else []) +
                                     ([t_ident] if prev_tq is None else []),
                                dma="dwb"))
        if rowtiles[-1] < 128:
            ld_tags.append(p.op("vector", lambda E, n_rt=n_rt, rowlen=rowlen:
                                E.memset(wbuf[96:128, (n_rt - 1) * rowlen:
                                              n_rt * rowlen], 0.0),
                                deps=[prev_tq] if prev_tq else []))
        Ftot = n_rt * rowlen
        slc_bounds = []
        o = 0
        while o < Ftot:
            w = min(1024, Ftot - o)
            slc_bounds.append((o, w))
            o += w
        tq = None
        for (o, w) in slc_bounds:
            tq = emit_quant(p, "vector", wbuf[:, o:o + w] if Ftot == 3072 else
                            arf[:, o:o + w],
                            wq_b[:, o:o + w], qbm[:, 0:w // 16],
                            qsc[:, 0:w // 16], qrc[:, 0:w // 16],
                            qslc[0][:, 0:w], qslc[1][:, 0:w], qslc[2][:, 0:w],
                            w, (ld_tags + [t_msk]) if o == 0 else [tq],
                            final_war=[prev_last_mm] if (o == 0 and
                                                         prev_last_mm) else [],
                            mbc=mskt[:, 0:1])
        prev_tq = tq
        ncc = rowlen // 128
        for rt, nr in enumerate(rowtiles):
            for g0 in range(0, ncc, 8):
                gn = min(8, ncc - g0)
                mm_tags = []
                for j in range(gn):
                    cc = g0 + j
                    def mkt(rt=rt, cc=cc, j=j, rowlen=rowlen):
                        return lambda E: E.transpose(
                            pDbf[:, j * 128:(j + 1) * 128],
                            wq_b[:, rt * rowlen + cc * 128:
                                 rt * rowlen + (cc + 1) * 128],
                            ident[:, :])
                    dd = []
                    if j == 0:
                        dd = [tq, t_ident, t_pzD] + t_wgin_writes[-1:]
                    mm_tags.append(p.op("tensor", mkt(), deps=dd))
                prev_last_mm = mm_tags[-1]
                tv = p.op("scalar", lambda E, gn=gn: E.activation(
                    tev[:, 0:gn * 128], pDbf[:, 0:gn * 128], AF.Copy),
                    deps=mm_tags)
                w_last_tv = tv
                def mkw(rt=rt, g0=g0, gn=gn, nr=nr, gbase=gbase, grs=grs):
                    gofs = gbase + g0 * 128 * grs + rt * 128
                    return lambda E: E.dma_start(
                        out=bass.AP(wgin, gofs,
                                    [[grs, 128], [128 * grs, gn], [1, nr]]),
                        in_=tev[:, 0:gn * 128].rearrange(
                            "p (g j) -> p g j", g=gn)[:, :, 0:nr])
                t_wgin_writes.append(p.op("sync", mkw(), deps=[tv],
                                               dma="dgw"))

    def _wcc(E):
        ins_ = E.collective_compute(
            "AllGather", AL.bypass, replica_groups=[list(range(8))],
            ins=[wgin.ap().rearrange("(a b) -> a b", b=2048).opt()],
            outs=[wgall.ap().rearrange("(a b) -> a b", b=2048).opt()])
        ins_.then_inc(sems["cc0"], 1)
        return ins_
    p.op("gpsimd", _wcc, deps=list(t_wgin_writes), inc=False)
    t_wcc = ("cc0", 0, 1, True)
    if dbg:
        t_wcc = p.op("sync", lambda E: E.dma_start(out=dbg_wg.ap(),
                                                   in_=wgall.ap()),
                     deps=[t_wcc], dma="dgw")


    def finish_truncated():
        bars = [p.barrier_tag(en) for en in ENGS]
        p.op("sync", lambda E: E.nop(), deps=bars, inc=False)
        p.emit(block, sems)
        ctx.close()
        return nc

    if stop_after == "W":
        return finish_truncated()

    t_wkv = []
    for c in range(8):
        def mk(c=c):
            return lambda E: E.dma_start(
                out=wbig[:, c * 2048:(c + 1) * 2048],
                in_=wqkvt[c * 128:(c + 1) * 128, D:3 * D])
        t_wkv.append(p.op("gpsimd", mk(), dma="dgp"))

    t_cadd1 = None
    t_czero = p.op("vector", lambda E: E.memset(uq[:, 0:1536], 0.0))
    t_vones1 = p.op("vector", lambda E: E.memset(
        vs1.rearrange("p (h c) -> p h c", c=65)[:, :, 64:65], 1.0))
    k_store = [None] * NSUB
    v_store = [None] * NSUB
    xnfm_store = [None] * NSUB
    xqfm_store = [None] * NSUB
    prev_kevac = [None, None]
    prev_vevac = [None, None]
    prev_q_tag = [prev_tq, prev_tq]
    xs_war = [prev_tq, prev_tq]
    pdbf_reader = [w_last_tv, None]
    xq_stage_reader = [None, None]
    last_mm_t2 = [None, None]
    half_evac = [None, None]
    par_bufs = [
        dict(xs=xs, xn=xn, xnb=xnb, xqb=xqb, tevp=tev[:, 0:1024],
             ks=ks[:, :], vs=vs[:, :], s1=st1[:, :], s2=st2[:, :],
             s3=st3[:, :], s4=st4[:, :], qa=qslc[0], qb=qslc[1],
             qc=qslc[2], bm=qbm[:, 0:64], sc=qsc[:, 0:64], rc=qrc[:, 0:64]),
        dict(xs=xs1, xn=xn1, xnb=xnb1, xqb=xqb1, tevp=tev1b, ks=ks1,
             vs=vs1, s1=st1b[:, :], s2=st2b[:, :], s3=st3b[:, :],
             s4=st4b[:, :], qa=qA1, qb=qB1, qc=qC1, bm=qbm2[:, 0:64],
             sc=qsc2[:, 0:64], rc=qrc2[:, 0:64]),
    ]

    for ip in range(NSUB // 2):
        ii = [2 * ip, 2 * ip + 1]
        BB = [par_bufs[0], par_bufs[1]]
        txs = []
        for z in range(2):
            txs.append(p.op("sync", lambda E, i=ii[z], B=BB[z]: E.dma_start(
                out=B["xs"], in_=x_in[i * 128:(i + 1) * 128, :]),
                deps=[xs_war[z]], dma="dck"))
        tld = p.stream_tag("dck")
        ts1_, ts2_, tmu_, tv2_, tm2a_, tm2_, tsq_, trs_, txn_, txnb_ = \
            [[None, None] for _ in range(10)]
        for z in range(2):
            ts1_[z] = p.op("vector", lambda E, B=BB[z]: E.reduce_sum(
                B["s1"], B["xs"], axis=AX), deps=[tld])
        for z in range(2):
            ts2_[z] = p.op("scalar", lambda E, B=BB[z]: E.activation(
                B["qa"], B["xs"], AF.Square, accum_out=B["s2"]),
                deps=[tld, prev_q_tag[z]])
        for z in range(2):
            tmu_[z] = p.op("vector", lambda E, B=BB[z]: E.tensor_scalar(
                B["s1"], B["s1"], 1.0 / D, None, AL.mult), deps=[ts1_[z]])
        for z in range(2):
            tv2_[z] = p.op("vector", lambda E, B=BB[z]: E.tensor_scalar(
                B["s2"], B["s2"], 1.0 / D, None, AL.mult), deps=[ts2_[z]])
        for z in range(2):
            tm2a_[z] = p.op("vector", lambda E, B=BB[z]: E.tensor_scalar(
                B["s4"], B["s1"], B["s1"][:, 0:1], None, AL.mult),
                deps=[tmu_[z]])
        for z in range(2):
            tm2_[z] = p.op("vector", lambda E, B=BB[z]: E.tensor_tensor(
                B["s3"], B["s2"], B["s4"], AL.subtract),
                deps=[tm2a_[z], tv2_[z]])
        for z in range(2):
            tsq_[z] = p.op("scalar", lambda E, B=BB[z]: E.activation(
                B["s4"], B["s3"], AF.Ln, bias=epsv[:, 0:1], scale=1.0),
                deps=[tm2_[z], t_eps])
        for z in range(2):
            trs_[z] = p.op("scalar", lambda E, B=BB[z]: E.activation(
                B["s3"], B["s4"], AF.Exp, scale=-0.5), deps=[tsq_[z]])
        for z in range(2):
            txn_[z] = p.op("vector", lambda E, B=BB[z]: E.tensor_scalar(
                B["xn"], B["xs"], B["s1"][:, 0:1], B["s3"][:, 0:1],
                AL.subtract, AL.mult), deps=[trs_[z], tmu_[z]])
        for z in range(2):
            txnb_[z] = p.op("vector", lambda E, B=BB[z]: E.tensor_scalar(
                B["xnb"], B["xs"], B["s1"][:, 0:1], B["s3"][:, 0:1],
                AL.subtract, AL.mult),
                deps=[trs_[z], tmu_[z]] + ([prev_last_mm] if ip == 0 else []))
        tev1_ = [None, None]
        for z in range(2):
            mm_t = []
            for c in range(8):
                def mkt(c=c, z=z, B=BB[z]):
                    return lambda E: E.transpose(
                        pDbf[:, z * 1024 + c * 128: z * 1024 + (c + 1) * 128],
                        B["xnb"][:, c * 128:(c + 1) * 128], ident[:, :])
                dd = []
                if c == 0:
                    dd = [txnb_[z], t_ident, pdbf_reader[z], t_pzD]
                mm_t.append(p.op("tensor", mkt(), deps=dd))
            tev1_[z] = p.op("scalar", lambda E, z=z, B=BB[z]: E.activation(
                B["tevp"], pDbf[:, z * 1024:z * 1024 + 1024], AF.Copy),
                deps=mm_t + (t_wgin_writes[-1:] if ip == 0 else
                             [xqfm_store[ii[z] - 2]]))
            pdbf_reader[z] = tev1_[z]
        for z in range(2):
            t_st1 = p.op("sync", lambda E, i=ii[z], B=BB[z]: E.dma_start(
                out=xnfm_d.rearrange("(c p) t -> p c t", p=128)[:, :,
                                     i * 128:(i + 1) * 128],
                in_=B["tevp"].rearrange("p (c j) -> p c j", c=8)),
                deps=[tev1_[z]], dma="dst_xn")
            xnfm_store[ii[z]] = t_st1
        kv_evac = [[None] * 4, [None] * 4]
        for blkz, gl, gh in [(0, 0, 1), (1, 0, 1), (0, 2, 3), (1, 2, 3)]:
            B = BB[blkz]
            for g in (gl, gh):
                half = g % 2
                mm_tags = []
                for c in range(8):
                    def mkm(g=g, c=c, half=half, B=B):
                        return lambda E: E.matmul(
                            pA[:, half * 512:(half + 1) * 512],
                            B["tevp"][:, c * 128:(c + 1) * 128],
                            wbig[:, c * 2048 + g * 512:
                                 c * 2048 + (g + 1) * 512],
                            start=(c == 0), stop=(c == 7))
                    dd = []
                    if c == 0:
                        dd = [tev1_[blkz], t_pzA] + (
                            t_wkv if (g == 0 and ip == 0 and blkz == 0)
                            else [])
                        if half_evac[half] is not None:
                            dd.append(half_evac[half])
                    mm_tags.append(p.op("tensor", mkm(), deps=dd))
                if g < 2:
                    tk = p.op("scalar",
                              lambda E, g=g, half=half, B=B: E.activation(
                                  B["ks"][:, g * 512:(g + 1) * 512],
                                  pA[:, half * 512:(half + 1) * 512],
                                  AF.Exp),
                              deps=[mm_tags[-1]] +
                              ([prev_kevac[blkz]] if (g == 0 and
                                                      prev_kevac[blkz])
                               else []))
                else:
                    def mkv(g=g, half=half, B=B):
                        return lambda E: E.activation(
                            B["vs"].rearrange("p (h c) -> p h c", c=65)[
                                :, (g - 2) * 8:(g - 2) * 8 + 8, 0:64],
                            pA[:, half * 512:(half + 1) * 512].rearrange(
                                "p (h e) -> p h e", e=64), AF.Copy)
                    tk = p.op("scalar", mkv(),
                              deps=[mm_tags[-1]] +
                              ([prev_vevac[blkz]] if (g == 2 and
                                                      prev_vevac[blkz])
                               else []))
                kv_evac[blkz][g] = tk
                half_evac[half] = tk
        for z in range(2):
            tks = p.op("sync", lambda E, i=ii[z], B=BB[z]: E.dma_start(
                out=k_d[i * 128:(i + 1) * 128, :], in_=B["ks"]),
                deps=[kv_evac[z][0], kv_evac[z][1]], dma="dst_k")
            tvs = p.op("sync", lambda E, i=ii[z], B=BB[z]: E.dma_start(
                out=v_d[i * 128:(i + 1) * 128, :], in_=B["vs"]),
                deps=[kv_evac[z][2], kv_evac[z][3], t_vones, t_vones1],
                dma="dst_v")
            k_store[ii[z]] = tks
            v_store[ii[z]] = tvs
            prev_kevac[z] = tks
            prev_vevac[z] = tvs
        cd_all = []
        for z in range(2):
            for pr in range(NPAIR):
                def mkc(pr=pr, z=z, B=BB[z]):
                    return lambda E: E.matmul(
                        pC[:, pair_off(pr):pair_off(pr) + 130],
                        B["ks"][:, pr * 128:(pr + 1) * 128],
                        B["vs"][:, pr * 130:(pr + 1) * 130],
                        start=(z == 0 and pr % 3 == 0),
                        stop=(z == 1 and pr in (2, 5, 7)))
                dd = []
                if z == 0 and pr == 0:
                    dd = (kv_evac[0] + kv_evac[1] +
                          [t_vones, t_vones1, t_pzC] +
                          ([t_cadd1] if t_cadd1 is not None else []))
                cd_all.append(p.op("tensor", mkc(), deps=dd))
        t_cadd1 = p.op("vector", lambda E: E.scalar_tensor_tensor(
            uq[:, 0:1536], pC.ap(), 1.0, uq[:, 0:1536], AL.mult, AL.add),
            deps=[cd_all[-1]] + ([t_cadd1] if t_cadd1 is not None else
                                 [t_czero]))
        jobs = []
        for z in range(2):
            B = BB[z]
            jobs.append(dict(x=B["xn"], q=B["xqb"], bm=B["bm"], sc=B["sc"],
                             rc=B["rc"], t1=B["qa"], t2=B["qb"], t3=B["qc"],
                             F=D, dep=[txn_[z], ts2_[z]],
                             final_war=([prev_last_mm] if ip == 0 else
                                        [last_mm_t2[z]] if last_mm_t2[z]
                                        else [])))
        tqs = emit_quant_multi(p, jobs, mskt[:, 0:1])
        for z in range(2):
            prev_q_tag[z] = tqs[z]
        for z in range(2):
            stage = (pDbf, 2048) if z == 0 else (pAbf, 0)
            mm_t2 = []
            for c in range(8):
                def mkt2(c=c, z=z, B=BB[z], stage=stage):
                    return lambda E: E.transpose(
                        stage[0][:, stage[1] + c * 128:
                                 stage[1] + (c + 1) * 128],
                        B["xqb"][:, c * 128:(c + 1) * 128], ident[:, :])
                dd = []
                if c == 0:
                    dd = [tqs[z]]
                    if xq_stage_reader[z] is not None:
                        dd.append(xq_stage_reader[z])
                    if z == 1:
                        dd += [kv_evac[0][2], kv_evac[0][3],
                               kv_evac[1][2], kv_evac[1][3]]
                mm_t2.append(p.op("tensor", mkt2(), deps=dd))
            last_mm_t2[z] = mm_t2[-1]
            tev2 = p.op("scalar", lambda E, z=z, B=BB[z], stage=stage:
                        E.activation(B["tevp"],
                                     stage[0][:, stage[1]:stage[1] + 1024],
                                     AF.Copy),
                        deps=mm_t2 + [xnfm_store[ii[z]]])
            xq_stage_reader[z] = tev2
            if z == 1:
                half_evac[0] = tev2
                half_evac[1] = tev2
            t_st2 = p.op("sync", lambda E, i=ii[z], B=BB[z]: E.dma_start(
                out=xqfm_d.rearrange("(c p) t -> p c t", p=128)[:, :,
                                     i * 128:(i + 1) * 128],
                in_=B["tevp"].rearrange("p (c j) -> p c j", c=8)),
                deps=[tev2], dma="dst_xq")
            xqfm_store[ii[z]] = t_st2
            xs_war[z] = txnb_[z]

    if stop_after == "1":
        return finish_truncated()

    t_cst = p.op("sync", lambda E: E.dma_start(out=cgin.ap(),
                                               in_=uq[:, 0:1536]),
                 deps=[t_cadd1], dma="dcg")
    t_cev = t_cadd1
    def _ccc(E):
        ins_ = E.collective_compute(
            "AllGather", AL.bypass, replica_groups=[list(range(8))],
            ins=[cgin.ap().opt()], outs=[cgall.ap().opt()])
        ins_.then_inc(sems["cc1"], 1)
        return ins_
    p.op("gpsimd", _ccc, deps=[t_cst], inc=False)
    t_ccc = ("cc1", 0, 1, True)
    if dbg:
        t_ccc = p.op("sync", lambda E: E.dma_start(out=dbg_cg.ap(),
                                                   in_=cgall.ap()),
                     deps=[t_ccc], dma="dcg")
    t_cini = p.op("vector", lambda E: E.memset(cstA[:, :], 0.0))
    t_off = t_cini
    for r in range(8):
        def mkld(r=r):
            return lambda E: E.dma_start(
                out=cstB[:, :], in_=cgall.ap()[r * 128:(r + 1) * 128, :])
        tl = p.op("sync", mkld(), deps=[t_ccc, t_off, t_cst], dma="dcg")
        def mkacc(r=r):
            return lambda E: E.scalar_tensor_tensor(
                cstA[:, :], cstB[:, :], mko[:, r:r + 1], cstA[:, :],
                AL.mult, AL.add)
        t_off = p.op("vector", mkacc(), deps=[tl, t_mask, t_off])

    if stop_after == "C":
        if dbg:
            p.op("sync", lambda E: E.dma_start(out=dbg_cst.ap(),
                                               in_=cstA[:, :]),
                 deps=[t_off], dma="dcg")
            p.op("sync", lambda E: E.dma_start(out=dbg_mk.ap(),
                                               in_=mko[:, :]),
                 deps=[t_mask], dma="dcg")
        return finish_truncated()

    t_wq = []
    tens_bar = p.barrier_tag("tensor")
    for c in range(8):
        def mk(c=c):
            return lambda E: E.dma_start(
                out=wbig[:, c * 1024:(c + 1) * 1024],
                in_=wqkvt[c * 128:(c + 1) * 128, 0:D])
        t_wq.append(p.op("gpsimd", mk(), dma="dgp",
                         deps=[tens_bar] if c == 0 else []))
    for c in range(8):
        def mk(c=c):
            return lambda E: E.dma_start(
                out=wbig[:, 8192 + c * 1024: 8192 + (c + 1) * 1024],
                in_=woutt[c * 128:(c + 1) * 128, :])
        t_wq.append(p.op("gpsimd", mk(), dma="dgp"))

    cst = [cstA, cstB]
    t_cprev = t_off
    t_prev_cadd = t_cev
    acc_outp = {}
    prev_u = None
    prev_afm_ev = None
    prev_outp = None
    t_prev_attn_evac = []
    t_acc_load = {}
    prev_kq = None
    prev_vq = None

    for q in range(NQT):
        i0 = q * 2
        tqx = p.op("sync", lambda E, q=q: E.dma_start(
            out=xnq[:, :].rearrange("p (c t) -> p c t", c=8),
            in_=xnfm_d.rearrange("(c p) t -> p c t", p=128)[:, :,
                                 q * QTT:(q + 1) * QTT]),
            deps=[p.stream_tag("dst_xn")] +
                 ([prev_u] if prev_u else []), dma="dqx")
        tacc = []
        for s in range(2):
            i = i0 + s
            def mka(i=i):
                return lambda E: E.dma_start(
                    out=acc[:, i * D:(i + 1) * D],
                    in_=x_in[i * 128:(i + 1) * 128, :])
            ta = p.op("sync", mka(), dma="dacc",
                      deps=[prev_outp] if prev_outp else [])
            tacc.append(ta)
            t_acc_load[i] = ta
        t_uex = [None] * 8
        for m in range(8):
            half = m % 2
            mm_tags = []
            for c in range(8):
                def mkm(m=m, c=c, half=half):
                    return lambda E: E.matmul(
                        pA[:, half * 512: half * 512 + QTT],
                        wbig[:, c * 1024 + m * 128: c * 1024 + (m + 1) * 128],
                        xnq[:, c * QTT:(c + 1) * QTT],
                        start=(c == 0), stop=(c == 7))
                dd = []
                if c == 0:
                    dd = [tqx]
                    if m == 0 and q == 0:
                        dd += t_wq
                        dd.append(half_evac[0])
                        dd.append(half_evac[1])
                    if m >= 2:
                        dd.append(t_uex[m - 2])
                    if m < 2 and prev_outp is not None:
                        dd.append(prev_outp)
                mm_tags.append(p.op("tensor", mkm(), deps=dd))
            t_uex[m] = p.op("scalar", lambda E, m=m, half=half: E.activation(
                uq[:, m * QTT:(m + 1) * QTT],
                pA[:, half * 512: half * 512 + QTT], AF.Exp),
                deps=[mm_tags[-1]] + ([t_cst] if q == 0 else []))
        prev_u = t_uex[-1]
        if stop_after == "A0":
            continue
        for s in range(2):
            i = i0 + s
            tkq = p.op("sync", lambda E, i=i: E.dma_start(
                out=kq[:, :], in_=k_d[i * 128:(i + 1) * 128, :]),
                deps=[p.stream_tag("dst_k")] +
                     ([prev_kq] if prev_kq else []),
                dma="dkq")
            tvq = p.op("sync", lambda E, i=i: E.dma_start(
                out=vq[:, :], in_=v_d[i * 128:(i + 1) * 128, :]),
                deps=[p.stream_tag("dst_v")] +
                     ([prev_vq] if prev_vq else []),
                dma="dvq")
            for b in range(2):
                n = s * 2 + b
                g = q * 4 + n
                cur = cst[g % 2]
                nxt = cst[(g + 1) % 2]
                attn_mm = []
                horder = [hh for hh in range(NH) if hh % 2 == 0] + \
                         [hh for hh in range(NH) if hh % 2 == 1]
                for h in horder:
                    pr = h // 2
                    par = h % 2
                    def mka_(h=h, pr=pr, par=par, n=n, cur=cur):
                        return lambda E: E.matmul(
                            pD[0:64, head_off(h):head_off(h) + 65],
                            uq[par * 64:par * 64 + 64,
                               pr * QTT + n * 64: pr * QTT + (n + 1) * 64],
                            cur[par * 64:par * 64 + 64,
                                pair_off(pr) + par * 65:
                                pair_off(pr) + par * 65 + 65],
                            start=True, stop=True)
                    dd = []
                    if h == 0:
                        dd = [t_cprev, t_uex[7]] + t_prev_attn_evac
                    attn_mm.append(p.op("tensor", mka_(), deps=dd))
                cd_mm = []
                for pr in range(NPAIR):
                    def mkc_(pr=pr, b=b):
                        return lambda E: E.matmul(
                            pC[:, pair_off(pr):pair_off(pr) + 130],
                            kq[b * 64:(b + 1) * 64,
                               pr * 128:(pr + 1) * 128],
                            vq[b * 64:(b + 1) * 64,
                               pr * 130:(pr + 1) * 130],
                            start=True, stop=True)
                    dd = []
                    if pr == 0:
                        dd = [tkq, tvq, t_prev_cadd]
                    cd_mm.append(p.op("tensor", mkc_(), deps=dd))
                t_cprev = p.op(
                    "vector",
                    lambda E, cur=cur, nxt=nxt: E.scalar_tensor_tensor(
                        nxt[:, :], pC.ap(), 1.0, cur[:, :], AL.mult, AL.add),
                    deps=[cd_mm[-1], t_cprev])
                t_prev_cadd = t_cprev
                tden = []
                for gi, (h0, hn) in enumerate(HEAD_GROUPS):
                    def mkd_(gi=gi, h0=h0, hn=hn):
                        base = 512 * gi
                        return lambda E: E.tensor_scalar(
                            dn[:, h0:h0 + hn],
                            pD.ap()[0:64, base:base + 65 * hn].rearrange(
                                "p (h c) -> p h c", c=65)[:, :,
                                                          64:65].squeeze(2),
                            1e-30, None, AL.max)
                    tden.append(p.op("vector", mkd_(), deps=[attn_mm[-1]]))
                trec = p.op("vector", lambda E: E.reciprocal(dq[:, :],
                                                             dn[:, :]),
                            deps=tden)
                ev_tags = []
                for gi, (h0, hn) in enumerate(HEAD_GROUPS):
                    def mke_(gi=gi, h0=h0, hn=hn, b=b):
                        base = 512 * gi
                        return lambda E: E.tensor_tensor(
                            atm[b * 64:(b + 1) * 64,
                                h0 * 64:(h0 + hn) * 64].rearrange(
                                    "p (h e) -> p h e", e=64),
                            pD.ap()[0:64, base:base + 65 * hn].rearrange(
                                "p (h c) -> p h c", c=65)[:, :, 0:64],
                            dq[:, h0:h0 + hn].unsqueeze(2).broadcast_to(
                                [64, hn, 64]), AL.mult)
                    ev_tags.append(p.op("vector", mke_(), deps=[trec]))
                t_prev_attn_evac = ev_tags
            prev_kq = t_prev_cadd
            prev_vq = t_prev_cadd
            if stop_after == "A1":
                continue
            mm_t = []
            for c in range(8):
                def mkt_(c=c):
                    return lambda E: E.transpose(
                        pAbf[:, c * 128:(c + 1) * 128],
                        atm[:, c * 128:(c + 1) * 128],
                        ident[:, :])
                dd = []
                if c == 0:
                    dd = list(t_prev_attn_evac)
                    if prev_afm_ev is not None:
                        dd.append(prev_afm_ev)
                    if prev_outp is not None:
                        dd.append(prev_outp)
                mm_t.append(p.op("tensor", mkt_(), deps=dd))
            tafm = p.op("vector", lambda E: E.tensor_copy(afm[:, :],
                                                          pAbf[:, 0:1024]),
                        deps=mm_t)
            prev_afm_ev = tafm
            mm_o = []
            for ng in range(2):
                for c in range(8):
                    def mko_(c=c, ng=ng):
                        return lambda E: E.matmul(
                            pA[:, ng * 512:(ng + 1) * 512],
                            afm[:, c * 128:(c + 1) * 128],
                            wbig[:, 8192 + c * 1024 + ng * 512:
                                 8192 + c * 1024 + (ng + 1) * 512],
                            start=(c == 0), stop=(c == 7))
                    dd = [tafm] if (c == 0 and ng == 0) else []
                    mm_o.append(p.op("tensor", mko_(), deps=dd))
            t_out = p.op("vector", lambda E, i=i: E.scalar_tensor_tensor(
                acc[:, i * D:(i + 1) * D], pA.ap(), 1.0,
                acc[:, i * D:(i + 1) * D], AL.mult, AL.add),
                deps=[mm_o[-1], p.stream_tag("dacc")])
            acc_outp[i] = t_out
            prev_outp = t_out

    if stop_after in ("A0", "A1"):
        return finish_truncated()
    if stop_after == "A":
        sta = []
        for i in range(NSUB):
            def mksa(i=i):
                return lambda E: E.dma_start(
                    out=out[i * 128:(i + 1) * 128, :],
                    in_=acc[:, i * D:(i + 1) * D])
            sta.append(p.op("sync", mksa(), deps=[acc_outp[i]],
                            dma="dout"))
        p.op("sync", lambda E: E.nop(), deps=sta, inc=False)
        return finish_truncated()

    def load_stripe(si, deps):
        s0, W = STRIPES[si]
        tags = []
        for dst, gb in [(wgs, 0), (wus, SHE)]:
            for r in range(8):
                lo = max(s0, r * WSH)
                hi = min(s0 + W, (r + 1) * WSH)
                if lo >= hi:
                    continue
                def mkl(r=r, lo=lo, hi=hi, dst=dst, gb=gb, s0=s0, W=W):
                    gofs = r * GINE + gb + (lo - r * WSH)
                    return lambda E: E.dma_start(
                        out=dst[:, :].rearrange("p (c w) -> p c w", w=768)[
                            :, :, lo - s0:hi - s0],
                        in_=bass.AP(wgall, gofs,
                                    [[WSH, 128], [128 * WSH, 8],
                                     [1, hi - lo]]))
                tags.append(p.op("sync", mkl(), deps=deps, dma="dstrip"))
        ncc = W // 128
        for cc in range(ncc):
            def mkd(cc=cc, s0=s0):
                gofs = 2 * SHE + (s0 + cc * 128) * DSH
                return lambda E: E.dma_start(
                    out=wds[:, cc * 1024:(cc + 1) * 1024].rearrange(
                        "p (r j) -> p r j", r=8),
                    in_=bass.AP(wgall, gofs,
                                [[DSH, 128], [GINE, 8], [1, DSH]]))
            tags.append(p.op("sync", mkd(), deps=deps, dma="dstrip"))
        return tags

    t_ws = load_stripe(0, [t_wcc])
    prev_hfm = [None, None]
    prev_dn_ev = None
    t_acc_fin = dict(acc_outp)
    tgs_p = t_prev_cadd
    th_p = t_prev_attn_evac[-1]
    last_gu_mm = [None, None]
    fb = [
        dict(xqt=xqt[:, :], gsil=gsil, hbuf=hbuf, hqb=hqb, hfm=hfm[:, :],
             bm=qbm[:, 0:48], sc=qsc[:, 0:48], rc=qrc[:, 0:48], f=fslc,
             stg=(pDbf, 2048)),
        dict(xqt=xqt1, gsil=gsil1, hbuf=hbuf1, hqb=hqb1, hfm=hfm1,
             bm=qbm3, sc=qsc3, rc=qrc3, f=f1,
             stg=(pDbf, 2048)),
    ]
    prev_rb = [None, None]
    prev_th = [None, None]

    for si in range(4):
        s0, W = STRIPES[si]
        ncc = W // 128
        ngroups = [(0, 512), (512, W - 512)] if W > 512 else [(0, W)]

        def emit_gu(j, z, gdeps, udeps):
            nonlocal last_gu_mm
            F = fb[z]
            txq = p.op("sync", lambda E, j=j, F=F: E.dma_start(
                out=F["xqt"].rearrange("p (c t) -> p c t", c=8),
                in_=xqfm_d.rearrange("(c p) t -> p c t", p=128)[:, :,
                                     j * 128:(j + 1) * 128]),
                deps=[p.stream_tag("dst_xq")] +
                     ([last_gu_mm[z]] if last_gu_mm[z] else
                      [prev_outp] if prev_outp else []), dma="dxqt")
            gmm, umm = [], []
            for (o0, nw) in ngroups:
                for c in range(8):
                    def mkg(c=c, o0=o0, nw=nw, W=W, F=F):
                        return lambda E: E.matmul(
                            pC[:, o0:o0 + nw],
                            F["xqt"][:, c * 128:(c + 1) * 128],
                            wgs[:, c * 768 + o0: c * 768 + o0 + nw],
                            start=(c == 0), stop=(c == 7))
                    dd = ([txq] + t_ws + gdeps) if (c == 0 and o0 == 0) else []
                    gmm.append(p.op("tensor", mkg(), deps=dd))
            for (o0, nw) in ngroups:
                for c in range(8):
                    def mku(c=c, o0=o0, nw=nw, W=W, F=F):
                        return lambda E: E.matmul(
                            pD[:, o0:o0 + nw],
                            F["xqt"][:, c * 128:(c + 1) * 128],
                            wus[:, c * 768 + o0: c * 768 + o0 + nw],
                            start=(c == 0), stop=(c == 7))
                    dd = udeps if (c == 0 and o0 == 0) else []
                    umm.append(p.op("tensor", mku(), deps=dd))
            last_gu_mm[z] = umm[-1]
            return gmm, umm

        guq = {}
        guq[0] = emit_gu(0, 0, [tgs_p], [th_p] if th_p else [])
        for jp in range(NSUB // 2):
            j0, j1 = 2 * jp, 2 * jp + 1
            gmm0, umm0 = guq[j0]
            tgs0 = p.op("scalar", lambda E, W=W, F=fb[0]: E.activation(
                F["gsil"][:, 0:W], pC.ap()[:, 0:W], AF.Silu),
                deps=[gmm0[-1]] + ([prev_th[0]] if prev_th[0] else []))
            th0 = p.op("vector", lambda E, W=W, F=fb[0]: E.tensor_tensor(
                F["hbuf"][:, 0:W], pD.ap()[:, 0:W], F["gsil"][:, 0:W],
                AL.mult),
                deps=[umm0[-1], tgs0] + ([prev_rb[0]] if prev_rb[0] else []))
            guq[j1] = emit_gu(j1, 1, [tgs0], [th0])
            gmm1, umm1 = guq[j1]
            tgs1 = p.op("scalar", lambda E, W=W, F=fb[1]: E.activation(
                F["gsil"][:, 0:W], pC.ap()[:, 0:W], AF.Silu),
                deps=[gmm1[-1]] + ([prev_th[1]] if prev_th[1] else []))
            th1 = p.op("vector", lambda E, W=W, F=fb[1]: E.tensor_tensor(
                F["hbuf"][:, 0:W], pD.ap()[:, 0:W], F["gsil"][:, 0:W],
                AL.mult),
                deps=[umm1[-1], tgs1] + ([prev_rb[1]] if prev_rb[1] else []))
            jobs = []
            for z, thz in ((0, th0), (1, th1)):
                F = fb[z]
                jobs.append(dict(x=F["hbuf"][:, 0:W], q=F["hqb"][:, 0:W],
                                 bm=F["bm"][:, 0:W // 16],
                                 sc=F["sc"][:, 0:W // 16],
                                 rc=F["rc"][:, 0:W // 16],
                                 t1=F["f"][0][:, 0:W], t2=F["f"][1][:, 0:W],
                                 t3=F["f"][2][:, 0:W], F=W, dep=[thz],
                                 final_war=[prev_hfm[z]] if prev_hfm[z]
                                 else []))
            tqhs = emit_quant_multi(p, jobs, mskt[:, 0:1])
            prev_rb = list(p.q_rb_tags)
            if jp + 1 < NSUB // 2:
                guq[2 * jp + 2] = emit_gu(2 * jp + 2, 0,
                                          [tgs0, prev_hfm[1] or tqhs[1]],
                                          [th0])
            thfs = [None, None]
            for z, (tqh, jz) in enumerate(((tqhs[0], j0), (tqhs[1], j1))):
                F = fb[z]
                stgap, stgo = F["stg"]
                mm_t = []
                for cc in range(ncc):
                    def mkt(cc=cc, F=F, stgap=stgap, stgo=stgo):
                        return lambda E: E.transpose(
                            stgap[:, stgo + cc * 128: stgo + (cc + 1) * 128],
                            F["hqb"][:, cc * 128:(cc + 1) * 128], ident[:, :])
                    dd = []
                    if cc == 0:
                        dd = [tqh]
                        for t_ in (prev_hfm[0], prev_hfm[1]):
                            if t_ is not None:
                                dd.append(t_)
                        if th_p is not None:
                            dd.append(th_p)
                            th_p = None
                        if z == 1 and thfs[0] is not None:
                            dd.append(thfs[0])
                    mm_t.append(p.op("tensor", mkt(), deps=dd))
                thf = p.op("scalar", lambda E, ncc=ncc, F=F, stgap=stgap,
                           stgo=stgo: E.activation(
                               F["hfm"][:, 0:ncc * 128],
                               stgap[:, stgo:stgo + ncc * 128], AF.Copy),
                           deps=mm_t)
                prev_hfm[z] = thf
                thfs[z] = thf
            for z, jz in ((0, j0), (1, j1)):
                F = fb[z]
                dmm = []
                for ng in range(2):
                    for cc in range(ncc):
                        def mkdn(cc=cc, ng=ng, ncc=ncc, F=F):
                            return lambda E: E.matmul(
                                pA[:, ng * 512:(ng + 1) * 512],
                                F["hfm"][:, cc * 128:(cc + 1) * 128],
                                wds[:, cc * 1024 + ng * 512:
                                     cc * 1024 + (ng + 1) * 512],
                                start=(cc == 0), stop=(cc == ncc - 1))
                        dd = []
                        if cc == 0 and ng == 0:
                            dd = [thfs[z]]
                            if prev_dn_ev is not None:
                                dd.append(prev_dn_ev)
                            elif prev_outp is not None:
                                dd.append(prev_outp)
                        dmm.append(p.op("tensor", mkdn(), deps=dd))
                tdn = p.op("vector", lambda E, j=jz: E.scalar_tensor_tensor(
                    acc[:, j * D:(j + 1) * D], pA.ap(), 1.0,
                    acc[:, j * D:(j + 1) * D], AL.mult, AL.add),
                    deps=[dmm[-1], t_acc_fin[jz]])
                prev_dn_ev = tdn
                t_acc_fin[jz] = tdn
            prev_th = [th0, th1]
        tgs_p = tgs1
        th_p = th1
        if si + 1 < 4:
            t_ws = load_stripe(si + 1, [p.barrier_tag("tensor")])
            guq[0] = emit_gu(0, 0, [tgs_p, prev_hfm[1] or tgs_p],
                             [th_p] if th_p else [])

    st_tags = []
    for i in range(NSUB):
        def mks(i=i):
            return lambda E: E.dma_start(out=out[i * 128:(i + 1) * 128, :],
                                         in_=acc[:, i * D:(i + 1) * D])
        st_tags.append(p.op("sync", mks(), deps=[t_acc_fin[i]],
                                dma="dout"))
    p.op("sync", lambda E: E.nop(), deps=st_tags, inc=False)

    p.emit(block, sems)
    ctx.close()
    return nc


_NC_CACHE = None


def kernel(L0_attn_x, L0_attn_W_qkv, L0_attn_W_out, L0_ffn_w_gate,
           L0_ffn_w_up, L0_ffn_w_down):
    global _NC_CACHE
    x = np.asarray(L0_attn_x, dtype=np.float32)
    wqkv = np.asarray(L0_attn_W_qkv, dtype=np.float32)
    wout = np.asarray(L0_attn_W_out, dtype=np.float32)
    wg = np.asarray(L0_ffn_w_gate, dtype=np.float32)
    wu = np.asarray(L0_ffn_w_up, dtype=np.float32)
    wd = np.asarray(L0_ffn_w_down, dtype=np.float32)

    wqkvt = np.ascontiguousarray(wqkv.T)
    woutt = np.ascontiguousarray(wout.T)

    in_maps = []
    for c in range(8):
        b, t0 = c // 4, (c % 4) * TC
        m = np.zeros(8, dtype=np.float32)
        m[(c // 4) * 4:c] = 1.0
        in_maps.append({
            "x_c": np.ascontiguousarray(x[b, t0:t0 + TC, :]),
            "wqkvt": wqkvt,
            "woutt": woutt,
            "wg_sh": np.ascontiguousarray(wg[c * WSH:(c + 1) * WSH, :]),
            "wu_sh": np.ascontiguousarray(wu[c * WSH:(c + 1) * WSH, :]),
            "wd_sh": np.ascontiguousarray(wd[c * DSH:(c + 1) * DSH, :]),
            "cmask": m,
            "crank": np.array([[c]], dtype=np.int32),
        })

    if _NC_CACHE is None:
        _NC_CACHE = build_core()
    res = run_bass_kernel_spmd(_NC_CACHE, in_maps, core_ids=list(range(8)))

    outp = np.empty((2, 8192, D), dtype=np.float32)
    for c in range(8):
        b, t0 = c // 4, (c % 4) * TC
        outp[b, t0:t0 + TC, :] = res.results[c]["out_c"]
    return outp


# version 26
# speedup vs baseline: 1.0069x; 1.0069x over previous
import numpy as np
import concourse.bass as bass
import concourse.mybir as mybir
from concourse.bass_utils import run_bass_kernel_spmd

F32 = mybir.dt.float32
BF16 = mybir.dt.bfloat16
I32 = mybir.dt.int32
AL = mybir.AluOpType
AF = mybir.ActivationFunctionType
AX = mybir.AxisListType.X

D = 1024
TC = 2048
NSUB = 16
NQT = 8
QTT = 256
NH = 16
HD = 64
NPAIR = 8
II = 2816
WSH = II // 8
DSH = D // 8
SHE = 1024 * WSH
GINE = 3 * SHE
STRIPES = [(0, 768), (768, 768), (1536, 640), (2176, 640)]


def head_off(h):
    return 512 * (h // 7) + 65 * (h % 7)


def pair_off(p):
    return 512 * (p // 3) + 130 * (p % 3)


HEAD_GROUPS = [(0, 7), (7, 7), (14, 2)]

MAGIC_H = float(np.float32(1.5 * 2**22))
ROUND_BIT = 0x00200000
MANT_MASK = -0x00400000

ENGS = ["sync", "gpsimd", "vector", "scalar", "tensor"]
DMA_STREAMS = ["dck", "dst_xn", "dst_xq", "dst_k", "dst_v", "dgw", "dcg",
               "dmask", "dcrk", "dqx", "dacc", "dkq", "dvq", "dxqt", "dxqt1",
               "dstrip",
               "dout", "dwb", "dgp"]
SEMS = ENGS + ["cc0", "cc1"] + DMA_STREAMS


class P:

    def __init__(self):
        self.items = {e: [] for e in ENGS}
        self.cnt = {e: 0 for e in ENGS}
        self.opidx = {e: 0 for e in ENGS}
        self.drained = {e: 0 for e in ENGS}
        self.waited = {e: {f: 0 for f in SEMS} for e in ENGS}
        self.scnt = {s: 0 for s in SEMS}

    def dep(self, e, d):
        if d is None:
            return
        de, didx, dval, ddma = d
        if de not in ENGS:
            if dval > self.waited[e][de]:
                self.items[e].append(("wait", (de, dval), []))
                self.waited[e][de] = dval
            return
        if de == e and not ddma:
            if didx > self.drained[e]:
                self.items[e].append(("drain", None, []))
                self.drained[e] = self.opidx[e]
        else:
            if dval > self.waited[e][de]:
                self.items[e].append(("wait", (de, dval), []))
                self.waited[e][de] = dval

    def op(self, e, fn, deps=(), dma=False, inc=True):
        best = {}
        need_drain = False
        drain_idx = 0
        for d in deps:
            if d is None:
                continue
            de, didx, dval, ddma = d
            if de == e and ddma is False and de in ENGS:
                need_drain = True
                drain_idx = max(drain_idx, didx)
            else:
                if dval > best.get(de, 0):
                    best[de] = dval
        for de, dval in best.items():
            self.dep(e, (de, 0, dval, True))
        if need_drain:
            self.dep(e, (e, drain_idx, 0, False))
        self.opidx[e] += 1
        idx = self.opidx[e]
        incs = []
        if isinstance(dma, str):
            self.scnt[dma] += 16
            incs = [(dma, 16)]
            self.items[e].append(("op", fn, incs))
            return (dma, idx, self.scnt[dma], True)
        if inc or dma:
            n = 16 if dma else 1
            self.cnt[e] += n
            incs = [(e, n)]
        self.items[e].append(("op", fn, incs))
        return (e, idx, self.cnt[e], dma)

    def barrier_tag(self, e):
        return (e, self.opidx[e], self.cnt[e], False)

    def stream_tag(self, s):
        return (s, 0, self.scnt[s], True)

    def emit(self, block, sems):
        def run(e, eng):
            for kind, a, incs in self.items[e]:
                if kind == "drain":
                    eng.drain()
                elif kind == "wait":
                    eng.wait_ge(sems[a[0]], a[1])
                else:
                    ins = a(eng)
                    for (se, n) in incs:
                        ins.then_inc(sems[se], n)

        block.sync(lambda eng: run("sync", eng))
        block.gpsimd(lambda eng: run("gpsimd", eng))
        block.vector(lambda eng: run("vector", eng))
        block.scalar(lambda eng: run("scalar", eng))
        block.tensor(lambda eng: run("tensor", eng))


def emit_quant(p, eng, x_ap, q_ap, bm, sc, rc, t1, t2, t3, F, dep,
               final_war=(), mbc=None):
    NB = F // 16
    xv = lambda ap: ap.rearrange("p (n b) -> p n b", b=16)
    V, G = "vector", "gpsimd"
    dep = list(dep)
    mbcb = mbc.unsqueeze(2).broadcast_to([128, 1, F]).squeeze(1)
    tr = p.op(V, lambda E: E.reduce_max(bm, xv(x_ap), axis=AX,
                                        apply_absolute_value=True), deps=dep)
    ts = p.op(G, lambda E: E.tensor_scalar(sc, bm, 6e-12, float(1.0 / 6.0),
                                           AL.max, AL.mult), deps=[tr])
    tc = p.op(V, lambda E: E.reciprocal(rc, sc), deps=[ts])
    rcb = rc.unsqueeze(2).broadcast_to([128, NB, 16])
    scb = sc.unsqueeze(2).broadcast_to([128, NB, 16])
    tb = p.op(G, lambda E: E.tensor_tensor(xv(t2), xv(x_ap), rcb, AL.mult),
              deps=[tc] + dep)
    p.q_rb_tag = tb
    rb = t2
    tk = p.op(V, lambda E: E.tensor_scalar(t1, rb, MAGIC_H, MAGIC_H,
                                           AL.add, AL.subtract), deps=[tb])
    tp_ = p.op(V, lambda E: E.tensor_scalar(t3.bitcast(I32), rb.bitcast(I32),
                                            ROUND_BIT, None, AL.add),
               deps=[tb])
    tq1 = p.op(V, lambda E: E.tensor_scalar(t3.bitcast(I32), t3.bitcast(I32),
                                            MANT_MASK, None, AL.bitwise_and),
               deps=[tp_])
    tzc = p.op(V, lambda E: E.tensor_scalar(t2, t3, -1.0, 1.0, AL.max, AL.min),
               deps=[tq1, tk, tp_])
    th_ = p.op(V, lambda E: E.tensor_scalar(t1, t1, -1.0, 1.0, AL.max, AL.min),
               deps=[tk])
    tm = p.op(G, lambda E: E.tensor_tensor(t2, t1, t2, AL.subtract),
              deps=[th_, tzc])
    tl = p.op(V, lambda E: E.tensor_tensor(t3, t2, t3, AL.add),
              deps=[tm, tq1])
    tf = p.op(G, lambda E: E.tensor_tensor(xv(q_ap), xv(t3), scb, AL.mult),
              deps=[tl] + list(final_war))
    return tf


def emit_quant_multi(p, jobs, mbc):
    V, G = "vector", "gpsimd"
    n = len(jobs)
    st = [dict() for _ in range(n)]
    xvs = []
    for j in jobs:
        F = j["F"]
        j["NB"] = F // 16
        xvs.append(None)

    def xv(ap):
        return ap.rearrange("p (n b) -> p n b", b=16)

    for idx, j in enumerate(jobs):
        st[idx]["r"] = p.op(V, (lambda j: lambda E: E.reduce_max(
            j["bm"], xv(j["x"]), axis=AX, apply_absolute_value=True))(j),
            deps=list(j["dep"]))
    for idx, j in enumerate(jobs):
        st[idx]["s"] = p.op(G, (lambda j: lambda E: E.tensor_scalar(
            j["sc"], j["bm"], 6e-12, float(1.0 / 6.0), AL.max, AL.mult))(j),
            deps=[st[idx]["r"]])
    for idx, j in enumerate(jobs):
        st[idx]["c"] = p.op(V, (lambda j: lambda E: E.reciprocal(
            j["rc"], j["sc"]))(j), deps=[st[idx]["s"]])
    for idx, j in enumerate(jobs):
        NB = j["NB"]
        rcb = j["rc"].unsqueeze(2).broadcast_to([128, NB, 16])
        st[idx]["b"] = p.op(G, (lambda j, rcb: lambda E: E.tensor_tensor(
            xv(j["t2"]), xv(j["x"]), rcb, AL.mult))(j, rcb),
            deps=[st[idx]["c"]] + list(j["dep"]))
        st[idx]["b_tag"] = st[idx]["b"]
    for idx, j in enumerate(jobs):
        st[idx]["k"] = p.op(V, (lambda j: lambda E: E.tensor_scalar(
            j["t1"], j["t2"], MAGIC_H, MAGIC_H, AL.add, AL.subtract))(j),
            deps=[st[idx]["b"]])
    for idx, j in enumerate(jobs):
        st[idx]["p"] = p.op(V, (lambda j: lambda E: E.tensor_scalar(
            j["t3"].bitcast(I32), j["t2"].bitcast(I32), ROUND_BIT, None,
            AL.add))(j), deps=[st[idx]["b"]])
    for idx, j in enumerate(jobs):
        st[idx]["q1"] = p.op(V, (lambda j: lambda E: E.tensor_scalar(
            j["t3"].bitcast(I32), j["t3"].bitcast(I32), MANT_MASK, None,
            AL.bitwise_and))(j), deps=[st[idx]["p"]])
    for idx, j in enumerate(jobs):
        st[idx]["zc"] = p.op(V, (lambda j: lambda E: E.tensor_scalar(
            j["t2"], j["t3"], -1.0, 1.0, AL.max, AL.min))(j),
            deps=[st[idx]["q1"], st[idx]["k"]])
    for idx, j in enumerate(jobs):
        st[idx]["hcl"] = p.op(V, (lambda j: lambda E: E.tensor_scalar(
            j["t1"], j["t1"], -1.0, 1.0, AL.max, AL.min))(j),
            deps=[st[idx]["k"]])
    for idx, j in enumerate(jobs):
        st[idx]["m"] = p.op(G, (lambda j: lambda E: E.tensor_tensor(
            j["t2"], j["t1"], j["t2"], AL.subtract))(j),
            deps=[st[idx]["hcl"], st[idx]["zc"]])
    for idx, j in enumerate(jobs):
        st[idx]["l"] = p.op(V, (lambda j: lambda E: E.tensor_tensor(
            j["t3"], j["t2"], j["t3"], AL.add))(j),
            deps=[st[idx]["m"], st[idx]["q1"]])
    outs = []
    for idx, j in enumerate(jobs):
        NB = j["NB"]
        scb = j["sc"].unsqueeze(2).broadcast_to([128, NB, 16])
        outs.append(p.op(G, (lambda j, scb: lambda E: E.tensor_tensor(
            xv(j["q"]), xv(j["t3"]), scb, AL.mult))(j, scb),
            deps=[st[idx]["l"]] + list(j["final_war"])))
    p.q_rb_tags = [st[i]["b_tag"] for i in range(n)]
    return outs


def build_core(stop_after=None, dbg=False):
    nc = bass.Bass("TRN2", target_bir_lowering=False)
    dkind = "ExternalOutput" if dbg else "Internal"

    x_in = nc.dram_tensor("x_c", [TC, D], F32, kind="ExternalInput").ap()
    wqkvt = nc.dram_tensor("wqkvt", [D, 3 * D], F32, kind="ExternalInput").ap()
    woutt = nc.dram_tensor("woutt", [D, D], F32, kind="ExternalInput").ap()
    wg_sh = nc.dram_tensor("wg_sh", [WSH, D], F32, kind="ExternalInput").ap()
    wu_sh = nc.dram_tensor("wu_sh", [WSH, D], F32, kind="ExternalInput").ap()
    wd_sh = nc.dram_tensor("wd_sh", [DSH, II], F32, kind="ExternalInput").ap()
    cmask = nc.dram_tensor("cmask", [8], F32, kind="ExternalInput").ap()
    crank = nc.dram_tensor("crank", [1, 1], I32, kind="ExternalInput").ap()
    out = nc.dram_tensor("out_c", [TC, D], F32, kind="ExternalOutput").ap()

    xnfm_d = nc.dram_tensor("xnfm_d", [D, TC], BF16, kind=dkind).ap()
    xqfm_d = nc.dram_tensor("xqfm_d", [D, TC], BF16, kind=dkind).ap()
    k_d = nc.dram_tensor("k_d", [TC, D], BF16, kind=dkind).ap()
    v_d = nc.dram_tensor("v_d", [TC, NH * 65], BF16, kind=dkind).ap()
    wgin = nc.dram_tensor("wgin", [GINE], BF16)
    wgall = nc.dram_tensor("wgall", [8 * GINE], BF16, addr_space="Shared")
    cgin = nc.dram_tensor("cgin", [128, 1536], F32)
    cgall = nc.dram_tensor("cgall", [8 * 128, 1536], F32, addr_space="Shared")
    if dbg:
        dbg_wg = nc.dram_tensor("dbg_wg", [8 * GINE], BF16,
                                kind="ExternalOutput")
        dbg_cg = nc.dram_tensor("dbg_cg", [8 * 128, 1536], F32,
                                kind="ExternalOutput")
        dbg_cst = nc.dram_tensor("dbg_cst", [128, 1536], F32,
                                 kind="ExternalOutput")
        dbg_mk = nc.dram_tensor("dbg_mk", [128, 8], F32,
                                kind="ExternalOutput")

    from contextlib import ExitStack
    ctx = ExitStack()
    e = ctx.enter_context

    acc = e(nc.sbuf_tensor("acc", [128, NSUB * D], F32))
    cstA = e(nc.sbuf_tensor("cstA", [128, 1536], F32))
    cstB = e(nc.sbuf_tensor("cstB", [128, 1536], F32))
    ident = e(nc.sbuf_tensor("ident", [128, 128], BF16))
    mko = e(nc.sbuf_tensor("mko", [128, 8], F32))

    arf = e(nc.sbuf_tensor("arf", [128, 6144], F32))
    arb = e(nc.sbuf_tensor("arb", [128, 3072], BF16))
    tev = e(nc.sbuf_tensor("tev", [128, 1024], BF16))
    wbig = e(nc.sbuf_tensor("wbig", [128, 8 * 2048], BF16))
    wgs = e(nc.sbuf_tensor("wgs", [128, 8 * 768], BF16))
    wus = e(nc.sbuf_tensor("wus", [128, 8 * 768], BF16))
    wds = e(nc.sbuf_tensor("wds", [128, 6 * 1024], BF16))

    qbm = e(nc.sbuf_tensor("qbm", [128, 64], F32))
    qsc = e(nc.sbuf_tensor("qsc", [128, 64], F32))
    qrc = e(nc.sbuf_tensor("qrc", [128, 64], F32))

    ks = e(nc.sbuf_tensor("ks", [128, D], BF16))
    vs = e(nc.sbuf_tensor("vs", [128, NH * 65], BF16))
    st1 = e(nc.sbuf_tensor("st1", [128, 1], F32))
    st2 = e(nc.sbuf_tensor("st2", [128, 1], F32))
    st3 = e(nc.sbuf_tensor("st3", [128, 1], F32))
    st4 = e(nc.sbuf_tensor("st4", [128, 1], F32))
    st1b = e(nc.sbuf_tensor("st1b", [128, 1], F32))
    st2b = e(nc.sbuf_tensor("st2b", [128, 1], F32))
    st3b = e(nc.sbuf_tensor("st3b", [128, 1], F32))
    st4b = e(nc.sbuf_tensor("st4b", [128, 1], F32))
    qbm2 = e(nc.sbuf_tensor("qbm2", [128, 64], F32))
    qsc2 = e(nc.sbuf_tensor("qsc2", [128, 64], F32))
    qrc2 = e(nc.sbuf_tensor("qrc2", [128, 64], F32))
    epsv = e(nc.sbuf_tensor("epsv", [128, 1], F32))
    mskt = e(nc.sbuf_tensor("mskt", [128, 1], I32))
    crk = e(nc.sbuf_tensor("crk", [1, 1], I32))
    rg_base = nc.sync.alloc_register("rgbase")
    rg_tmp = nc.sync.alloc_register("rgtmp")

    xnq = e(nc.sbuf_tensor("xnq", [128, 8 * QTT], BF16))
    uq = e(nc.sbuf_tensor("uq", [128, 8 * QTT], F32))
    kq = e(nc.sbuf_tensor("kq", [128, D], BF16))
    vq = e(nc.sbuf_tensor("vq", [128, NH * 65], BF16))
    atm = e(nc.sbuf_tensor("atm", [128, D], BF16))
    afm = e(nc.sbuf_tensor("afm", [128, 8 * 128], BF16))
    dq = e(nc.sbuf_tensor("dq", [64, 16], F32))
    dn = e(nc.sbuf_tensor("dn", [64, 16], F32))

    xqt = e(nc.sbuf_tensor("xqt", [128, 8 * 128], BF16))
    hfm = e(nc.sbuf_tensor("hfm", [128, 6 * 128], BF16))

    iwork = arf[:, 0:128].bitcast(I32)
    wbuf = arf[:, 0:3072]
    qslc = [arf[:, 3072:4096], arf[:, 4096:5120], arf[:, 5120:6144]]
    xs = arf[:, 0:1024]
    xn = arf[:, 1024:2048]
    hbuf = arf[:, 0:768]
    fslc = [arf[:, 3072 + 1024 * i: 3072 + 1024 * i + 768] for i in range(3)]
    wq_b = arb[:, 0:3072]
    xnb = arb[:, 0:1024]
    xqb = arb[:, 1024:2048]
    gsil = arb[:, 0:768]
    hqb = arb[:, 1024:1792]
    wgs_f = wgs.ap().bitcast(F32)
    wus_f = wus.ap().bitcast(F32)
    xs1 = wgs_f[:, 0:1024]
    xn1 = wgs_f[:, 1024:2048]
    qA1 = wgs_f[:, 2048:3072]
    qB1 = wus_f[:, 0:1024]
    qC1 = wus_f[:, 1024:2048]
    wbig_f = wbig.ap().bitcast(F32)
    hbuf1 = wbig_f[:, 0:768]
    f1 = [wbig_f[:, 768:1536], wbig_f[:, 1536:2304], wbig_f[:, 2304:3072]]
    qbm3 = wbig_f[:, 3072:3120]
    qsc3 = wbig_f[:, 3120:3168]
    qrc3 = wbig_f[:, 3168:3216]
    wbig_b = wbig[:, :]
    gsil1 = wbig_b[:, 6528:7296]
    hqb1 = wbig_b[:, 7296:8064]
    hfm1 = wbig_b[:, 8064:8832]
    xqt1 = wbig_b[:, 8832:9856]
    xnb1 = wds[:, 0:1024]
    xqb1 = wds[:, 1024:2048]
    tev1b = wds[:, 2048:3072]
    ks1 = wds[:, 3072:4096]
    vs1 = wds[:, 4096:5136]

    pA = e(nc.psum_tensor("pA", [128, 1024], F32))
    pC = e(nc.psum_tensor("pC", [128, 1536], F32))
    pD = e(nc.psum_tensor("pD", [128, 1536], F32))
    pAbf = pA.ap().bitcast(BF16)
    pCbf = pC.ap().bitcast(BF16)
    pDbf = pD.ap().bitcast(BF16)

    sems = {}
    for name in SEMS:
        sems[name] = e(nc.semaphore("s_" + name))
    block = e(nc.Block())

    p = P()

    t_id0 = p.op("gpsimd", lambda E: E.iota(iwork, pattern=[[1, 128]],
                                            base=0, channel_multiplier=-1))
    t_id1 = p.op("gpsimd", lambda E: E.tensor_scalar(iwork, iwork,
                                                     0, None, AL.is_equal),
                 deps=[t_id0])
    t_ident = p.op("gpsimd", lambda E: E.tensor_copy(ident[:, :], iwork),
                   deps=[t_id1])
    t_mask = p.op("sync", lambda E: E.dma_start(
        out=mko[:, :], in_=cmask.unsqueeze(0).broadcast_to([128, 8])),
        dma="dmask")
    t_vones = p.op("vector", lambda E: E.memset(
        vs[:, :].rearrange("p (h c) -> p h c", c=65)[:, :, 64:65], 1.0))
    t_eps = p.op("vector", lambda E: E.memset(epsv[:, :], 1e-5))
    t_msk = p.op("vector", lambda E: E.memset(mskt[:, :], MANT_MASK))
    t_crk = p.op("sync", lambda E: E.dma_start(out=crk[:, :], in_=crank),
                 dma="dcrk")
    def _ldrank(E):
        ins = E.reg_load(rg_base, crk[0:1, 0:1])
        E.reg_mul(rg_base, rg_base, GINE)
        return ins
    p.op("sync", _ldrank, deps=[t_crk], inc=False)
    t_pzA = p.op("vector", lambda E: E.memset(pA.ap(), 0.0))
    t_pzC = p.op("vector", lambda E: E.memset(pC.ap(), 0.0))
    t_pzD = p.op("vector", lambda E: E.memset(pD.ap(), 0.0))

    wshard_specs = [
        (wg_sh, [128, 128, 96], D, 0, WSH),
        (wu_sh, [128, 128, 96], D, SHE, WSH),
        (wd_sh, [128], II, 2 * SHE, DSH),
    ]
    t_wgin_writes = []
    prev_tq = None
    prev_last_mm = None
    w_last_tv = None
    for din, rowtiles, rowlen, gbase, grs in wshard_specs:
        n_rt = len(rowtiles)
        ld_tags = []
        for rt, nr in enumerate(rowtiles):
            def mkld(rt=rt, nr=nr, din=din, rowlen=rowlen):
                return lambda E: E.dma_start(
                    out=wbuf[0:nr, rt * rowlen:(rt + 1) * rowlen]
                    if rowlen == D else wbuf[0:nr, 0:rowlen],
                    in_=din[rt * 128: rt * 128 + nr, :])
            ld_tags.append(p.op("sync", mkld(),
                                deps=([prev_tq] if rt == 0 else []) +
                                     ([t_ident] if prev_tq is None else []),
                                dma="dwb"))
        if rowtiles[-1] < 128:
            ld_tags.append(p.op("vector", lambda E, n_rt=n_rt, rowlen=rowlen:
                                E.memset(wbuf[96:128, (n_rt - 1) * rowlen:
                                              n_rt * rowlen], 0.0),
                                deps=[prev_tq] if prev_tq else []))
        Ftot = n_rt * rowlen
        slc_bounds = []
        o = 0
        while o < Ftot:
            w = min(1024, Ftot - o)
            slc_bounds.append((o, w))
            o += w
        tq = None
        for (o, w) in slc_bounds:
            tq = emit_quant(p, "vector", wbuf[:, o:o + w] if Ftot == 3072 else
                            arf[:, o:o + w],
                            wq_b[:, o:o + w], qbm[:, 0:w // 16],
                            qsc[:, 0:w // 16], qrc[:, 0:w // 16],
                            qslc[0][:, 0:w], qslc[1][:, 0:w], qslc[2][:, 0:w],
                            w, (ld_tags + [t_msk]) if o == 0 else [tq],
                            final_war=[prev_last_mm] if (o == 0 and
                                                         prev_last_mm) else [],
                            mbc=mskt[:, 0:1])
        prev_tq = tq
        ncc = rowlen // 128
        for rt, nr in enumerate(rowtiles):
            for g0 in range(0, ncc, 8):
                gn = min(8, ncc - g0)
                mm_tags = []
                for j in range(gn):
                    cc = g0 + j
                    def mkt(rt=rt, cc=cc, j=j, rowlen=rowlen):
                        return lambda E: E.transpose(
                            pDbf[:, j * 128:(j + 1) * 128],
                            wq_b[:, rt * rowlen + cc * 128:
                                 rt * rowlen + (cc + 1) * 128],
                            ident[:, :])
                    dd = []
                    if j == 0:
                        dd = [tq, t_ident, t_pzD] + t_wgin_writes[-1:]
                    mm_tags.append(p.op("tensor", mkt(), deps=dd))
                prev_last_mm = mm_tags[-1]
                tv = p.op("scalar", lambda E, gn=gn: E.activation(
                    tev[:, 0:gn * 128], pDbf[:, 0:gn * 128], AF.Copy),
                    deps=mm_tags)
                w_last_tv = tv
                def mkw(rt=rt, g0=g0, gn=gn, nr=nr, gbase=gbase, grs=grs):
                    gofs = gbase + g0 * 128 * grs + rt * 128
                    return lambda E: E.dma_start(
                        out=bass.AP(wgin, gofs,
                                    [[grs, 128], [128 * grs, gn], [1, nr]]),
                        in_=tev[:, 0:gn * 128].rearrange(
                            "p (g j) -> p g j", g=gn)[:, :, 0:nr])
                t_wgin_writes.append(p.op("sync", mkw(), deps=[tv],
                                               dma="dgw"))

    def _wcc(E):
        ins_ = E.collective_compute(
            "AllGather", AL.bypass, replica_groups=[list(range(8))],
            ins=[wgin.ap().rearrange("(a b) -> a b", b=2048).opt()],
            outs=[wgall.ap().rearrange("(a b) -> a b", b=2048).opt()])
        ins_.then_inc(sems["cc0"], 1)
        return ins_
    p.op("gpsimd", _wcc, deps=list(t_wgin_writes), inc=False)
    t_wcc = ("cc0", 0, 1, True)
    if dbg:
        t_wcc = p.op("sync", lambda E: E.dma_start(out=dbg_wg.ap(),
                                                   in_=wgall.ap()),
                     deps=[t_wcc], dma="dgw")


    def finish_truncated():
        bars = [p.barrier_tag(en) for en in ENGS]
        p.op("sync", lambda E: E.nop(), deps=bars, inc=False)
        p.emit(block, sems)
        ctx.close()
        return nc

    if stop_after == "W":
        return finish_truncated()

    t_wkv = []
    for c in range(8):
        def mk(c=c):
            return lambda E: E.dma_start(
                out=wbig[:, c * 2048:(c + 1) * 2048],
                in_=wqkvt[c * 128:(c + 1) * 128, D:3 * D])
        t_wkv.append(p.op("gpsimd", mk(), dma="dgp"))

    t_cadd1 = None
    t_czero = p.op("vector", lambda E: E.memset(uq[:, 0:1536], 0.0))
    t_vones1 = p.op("vector", lambda E: E.memset(
        vs1.rearrange("p (h c) -> p h c", c=65)[:, :, 64:65], 1.0))
    k_store = [None] * NSUB
    v_store = [None] * NSUB
    xnfm_store = [None] * NSUB
    xqfm_store = [None] * NSUB
    prev_kevac = [None, None]
    prev_vevac = [None, None]
    prev_q_tag = [prev_tq, prev_tq]
    xs_war = [prev_tq, prev_tq]
    pdbf_reader = [w_last_tv, None]
    xq_stage_reader = [None, None]
    last_mm_t2 = [None, None]
    half_evac = [None, None]
    par_bufs = [
        dict(xs=xs, xn=xn, xnb=xnb, xqb=xqb, tevp=tev[:, 0:1024],
             ks=ks[:, :], vs=vs[:, :], s1=st1[:, :], s2=st2[:, :],
             s3=st3[:, :], s4=st4[:, :], qa=qslc[0], qb=qslc[1],
             qc=qslc[2], bm=qbm[:, 0:64], sc=qsc[:, 0:64], rc=qrc[:, 0:64]),
        dict(xs=xs1, xn=xn1, xnb=xnb1, xqb=xqb1, tevp=tev1b, ks=ks1,
             vs=vs1, s1=st1b[:, :], s2=st2b[:, :], s3=st3b[:, :],
             s4=st4b[:, :], qa=qA1, qb=qB1, qc=qC1, bm=qbm2[:, 0:64],
             sc=qsc2[:, 0:64], rc=qrc2[:, 0:64]),
    ]

    for ip in range(NSUB // 2):
        ii = [2 * ip, 2 * ip + 1]
        BB = [par_bufs[0], par_bufs[1]]
        txs = []
        for z in range(2):
            txs.append(p.op("sync", lambda E, i=ii[z], B=BB[z]: E.dma_start(
                out=B["xs"], in_=x_in[i * 128:(i + 1) * 128, :]),
                deps=[xs_war[z]], dma="dck"))
        tld = p.stream_tag("dck")
        ts1_, ts2_, tmu_, tv2_, tm2a_, tm2_, tsq_, trs_, txn_, txnb_ = \
            [[None, None] for _ in range(10)]
        for z in range(2):
            ts1_[z] = p.op("vector", lambda E, B=BB[z]: E.reduce_sum(
                B["s1"], B["xs"], axis=AX), deps=[tld])
        for z in range(2):
            ts2_[z] = p.op("scalar", lambda E, B=BB[z]: E.activation(
                B["qa"], B["xs"], AF.Square, accum_out=B["s2"]),
                deps=[tld, prev_q_tag[z]])
        for z in range(2):
            tmu_[z] = p.op("vector", lambda E, B=BB[z]: E.tensor_scalar(
                B["s1"], B["s1"], 1.0 / D, None, AL.mult), deps=[ts1_[z]])
        for z in range(2):
            tv2_[z] = p.op("vector", lambda E, B=BB[z]: E.tensor_scalar(
                B["s2"], B["s2"], 1.0 / D, None, AL.mult), deps=[ts2_[z]])
        for z in range(2):
            tm2a_[z] = p.op("vector", lambda E, B=BB[z]: E.tensor_scalar(
                B["s4"], B["s1"], B["s1"][:, 0:1], None, AL.mult),
                deps=[tmu_[z]])
        for z in range(2):
            tm2_[z] = p.op("vector", lambda E, B=BB[z]: E.tensor_tensor(
                B["s3"], B["s2"], B["s4"], AL.subtract),
                deps=[tm2a_[z], tv2_[z]])
        for z in range(2):
            tsq_[z] = p.op("scalar", lambda E, B=BB[z]: E.activation(
                B["s4"], B["s3"], AF.Ln, bias=epsv[:, 0:1], scale=1.0),
                deps=[tm2_[z], t_eps])
        for z in range(2):
            trs_[z] = p.op("scalar", lambda E, B=BB[z]: E.activation(
                B["s3"], B["s4"], AF.Exp, scale=-0.5), deps=[tsq_[z]])
        for z in range(2):
            txn_[z] = p.op("vector", lambda E, B=BB[z]: E.tensor_scalar(
                B["xn"], B["xs"], B["s1"][:, 0:1], B["s3"][:, 0:1],
                AL.subtract, AL.mult), deps=[trs_[z], tmu_[z]])
        for z in range(2):
            txnb_[z] = p.op("vector", lambda E, B=BB[z]: E.tensor_scalar(
                B["xnb"], B["xs"], B["s1"][:, 0:1], B["s3"][:, 0:1],
                AL.subtract, AL.mult),
                deps=[trs_[z], tmu_[z]] + ([prev_last_mm] if ip == 0 else []))
        tev1_ = [None, None]
        for z in range(2):
            mm_t = []
            for c in range(8):
                def mkt(c=c, z=z, B=BB[z]):
                    return lambda E: E.transpose(
                        pDbf[:, z * 1024 + c * 128: z * 1024 + (c + 1) * 128],
                        B["xnb"][:, c * 128:(c + 1) * 128], ident[:, :])
                dd = []
                if c == 0:
                    dd = [txnb_[z], t_ident, pdbf_reader[z], t_pzD]
                mm_t.append(p.op("tensor", mkt(), deps=dd))
            tev1_[z] = p.op("scalar", lambda E, z=z, B=BB[z]: E.activation(
                B["tevp"], pDbf[:, z * 1024:z * 1024 + 1024], AF.Copy),
                deps=mm_t + (t_wgin_writes[-1:] if ip == 0 else
                             [xqfm_store[ii[1] - 2]]))
            pdbf_reader[z] = tev1_[z]
        for z in range(2):
            t_st1 = p.op("sync", lambda E, i=ii[z], B=BB[z]: E.dma_start(
                out=xnfm_d.rearrange("(c p) t -> p c t", p=128)[:, :,
                                     i * 128:(i + 1) * 128],
                in_=B["tevp"].rearrange("p (c j) -> p c j", c=8)),
                deps=[tev1_[z]], dma="dst_xn")
            xnfm_store[ii[z]] = t_st1
        kv_evac = [[None] * 4, [None] * 4]
        for blkz, gl, gh in [(0, 0, 1), (1, 0, 1), (0, 2, 3), (1, 2, 3)]:
            B = BB[blkz]
            for g in (gl, gh):
                half = g % 2
                mm_tags = []
                for c in range(8):
                    def mkm(g=g, c=c, half=half, B=B):
                        return lambda E: E.matmul(
                            pA[:, half * 512:(half + 1) * 512],
                            B["tevp"][:, c * 128:(c + 1) * 128],
                            wbig[:, c * 2048 + g * 512:
                                 c * 2048 + (g + 1) * 512],
                            start=(c == 0), stop=(c == 7))
                    dd = []
                    if c == 0:
                        dd = [tev1_[blkz], t_pzA] + (
                            t_wkv if (g == 0 and ip == 0 and blkz == 0)
                            else [])
                        if half_evac[half] is not None:
                            dd.append(half_evac[half])
                    mm_tags.append(p.op("tensor", mkm(), deps=dd))
                if g < 2:
                    tk = p.op("scalar",
                              lambda E, g=g, half=half, B=B: E.activation(
                                  B["ks"][:, g * 512:(g + 1) * 512],
                                  pA[:, half * 512:(half + 1) * 512],
                                  AF.Exp),
                              deps=[mm_tags[-1]] +
                              ([prev_kevac[blkz]] if (g == 0 and
                                                      prev_kevac[blkz])
                               else []))
                else:
                    def mkv(g=g, half=half, B=B):
                        return lambda E: E.activation(
                            B["vs"].rearrange("p (h c) -> p h c", c=65)[
                                :, (g - 2) * 8:(g - 2) * 8 + 8, 0:64],
                            pA[:, half * 512:(half + 1) * 512].rearrange(
                                "p (h e) -> p h e", e=64), AF.Copy)
                    tk = p.op("scalar", mkv(),
                              deps=[mm_tags[-1]] +
                              ([prev_vevac[blkz]] if (g == 2 and
                                                      prev_vevac[blkz])
                               else []))
                kv_evac[blkz][g] = tk
                half_evac[half] = tk
        for z in range(2):
            tks = p.op("sync", lambda E, i=ii[z], B=BB[z]: E.dma_start(
                out=k_d[i * 128:(i + 1) * 128, :], in_=B["ks"]),
                deps=[kv_evac[z][0], kv_evac[z][1]], dma="dst_k")
            tvs = p.op("sync", lambda E, i=ii[z], B=BB[z]: E.dma_start(
                out=v_d[i * 128:(i + 1) * 128, :], in_=B["vs"]),
                deps=[kv_evac[z][2], kv_evac[z][3], t_vones, t_vones1],
                dma="dst_v")
            k_store[ii[z]] = tks
            v_store[ii[z]] = tvs
            prev_kevac[z] = tks
            prev_vevac[z] = tvs
        prev_kevac[0] = prev_kevac[1]
        prev_vevac[0] = prev_vevac[1]
        cd_all = []
        for z in range(2):
            for pr in range(NPAIR):
                def mkc(pr=pr, z=z, B=BB[z]):
                    return lambda E: E.matmul(
                        pC[:, pair_off(pr):pair_off(pr) + 130],
                        B["ks"][:, pr * 128:(pr + 1) * 128],
                        B["vs"][:, pr * 130:(pr + 1) * 130],
                        start=(z == 0 and pr % 3 == 0),
                        stop=(z == 1 and pr in (2, 5, 7)))
                dd = []
                if z == 0 and pr == 0:
                    dd = (kv_evac[0] + kv_evac[1] +
                          [t_vones, t_vones1, t_pzC] +
                          ([t_cadd1] if t_cadd1 is not None else []))
                cd_all.append(p.op("tensor", mkc(), deps=dd))
        t_cadd1 = p.op("vector", lambda E: E.scalar_tensor_tensor(
            uq[:, 0:1536], pC.ap(), 1.0, uq[:, 0:1536], AL.mult, AL.add),
            deps=[cd_all[-1]] + ([t_cadd1] if t_cadd1 is not None else
                                 [t_czero]))
        jobs = []
        for z in range(2):
            B = BB[z]
            jobs.append(dict(x=B["xn"], q=B["xqb"], bm=B["bm"], sc=B["sc"],
                             rc=B["rc"], t1=B["qa"], t2=B["qb"], t3=B["qc"],
                             F=D, dep=[txn_[z], ts2_[z]],
                             final_war=([prev_last_mm] if ip == 0 else
                                        [last_mm_t2[z]] if last_mm_t2[z]
                                        else [])))
        tqs = emit_quant_multi(p, jobs, mskt[:, 0:1])
        for z in range(2):
            prev_q_tag[z] = tqs[z]
        for z in range(2):
            stage = (pDbf, 2048) if z == 0 else (pAbf, 0)
            mm_t2 = []
            for c in range(8):
                def mkt2(c=c, z=z, B=BB[z], stage=stage):
                    return lambda E: E.transpose(
                        stage[0][:, stage[1] + c * 128:
                                 stage[1] + (c + 1) * 128],
                        B["xqb"][:, c * 128:(c + 1) * 128], ident[:, :])
                dd = []
                if c == 0:
                    dd = [tqs[z]]
                    if xq_stage_reader[z] is not None:
                        dd.append(xq_stage_reader[z])
                    if z == 1:
                        dd += [kv_evac[0][2], kv_evac[0][3],
                               kv_evac[1][2], kv_evac[1][3]]
                mm_t2.append(p.op("tensor", mkt2(), deps=dd))
            last_mm_t2[z] = mm_t2[-1]
            tev2 = p.op("scalar", lambda E, z=z, B=BB[z], stage=stage:
                        E.activation(B["tevp"],
                                     stage[0][:, stage[1]:stage[1] + 1024],
                                     AF.Copy),
                        deps=mm_t2 + [xnfm_store[ii[1]]])
            xq_stage_reader[z] = tev2
            if z == 1:
                half_evac[0] = tev2
                half_evac[1] = tev2
            t_st2 = p.op("sync", lambda E, i=ii[z], B=BB[z]: E.dma_start(
                out=xqfm_d.rearrange("(c p) t -> p c t", p=128)[:, :,
                                     i * 128:(i + 1) * 128],
                in_=B["tevp"].rearrange("p (c j) -> p c j", c=8)),
                deps=[tev2], dma="dst_xq")
            xqfm_store[ii[z]] = t_st2
            xs_war[z] = txnb_[z]

    if stop_after == "1":
        return finish_truncated()

    t_cst = p.op("sync", lambda E: E.dma_start(out=cgin.ap(),
                                               in_=uq[:, 0:1536]),
                 deps=[t_cadd1], dma="dcg")
    t_cev = t_cadd1
    def _ccc(E):
        ins_ = E.collective_compute(
            "AllGather", AL.bypass, replica_groups=[list(range(8))],
            ins=[cgin.ap().opt()], outs=[cgall.ap().opt()])
        ins_.then_inc(sems["cc1"], 1)
        return ins_
    p.op("gpsimd", _ccc, deps=[t_cst], inc=False)
    t_ccc = ("cc1", 0, 1, True)
    if dbg:
        t_ccc = p.op("sync", lambda E: E.dma_start(out=dbg_cg.ap(),
                                                   in_=cgall.ap()),
                     deps=[t_ccc], dma="dcg")

    def load_stripe(si, deps):
        s0, W = STRIPES[si]
        tags = []
        for dst, gb in [(wgs, 0), (wus, SHE)]:
            for r in range(8):
                lo = max(s0, r * WSH)
                hi = min(s0 + W, (r + 1) * WSH)
                if lo >= hi:
                    continue
                def mkl(r=r, lo=lo, hi=hi, dst=dst, gb=gb, s0=s0, W=W):
                    gofs = r * GINE + gb + (lo - r * WSH)
                    return lambda E: E.dma_start(
                        out=dst[:, :].rearrange("p (c w) -> p c w", w=768)[
                            :, :, lo - s0:hi - s0],
                        in_=bass.AP(wgall, gofs,
                                    [[WSH, 128], [128 * WSH, 8],
                                     [1, hi - lo]]))
                tags.append(p.op("sync", mkl(), deps=deps, dma="dstrip"))
        ncc = W // 128
        for cc in range(ncc):
            def mkd(cc=cc, s0=s0):
                gofs = 2 * SHE + (s0 + cc * 128) * DSH
                return lambda E: E.dma_start(
                    out=wds[:, cc * 1024:(cc + 1) * 1024].rearrange(
                        "p (r j) -> p r j", r=8),
                    in_=bass.AP(wgall, gofs,
                                [[DSH, 128], [GINE, 8], [1, DSH]]))
            tags.append(p.op("sync", mkd(), deps=deps, dma="dstrip"))
        return tags

    tgs_p = t_cadd1
    th_p = xq_stage_reader[0]
    b_pa_war = [half_evac[0], half_evac[1], xq_stage_reader[1]]
    b_carve_bar = p.barrier_tag("tensor")
    t_ws = load_stripe(0, [t_wcc, b_carve_bar,
                           p.barrier_tag("vector"), p.barrier_tag("gpsimd"),
                           p.stream_tag("dst_xn"), p.stream_tag("dst_xq"),
                           p.stream_tag("dst_k"), p.stream_tag("dst_v")] +
                       b_pa_war)
    for i in range(NSUB):
        def mka(i=i):
            return lambda E: E.dma_start(
                out=acc[:, i * D:(i + 1) * D],
                in_=x_in[i * 128:(i + 1) * 128, :])
        p.op("sync", mka(), dma="dacc")
    t_accs = p.stream_tag("dacc")
    prev_hfm = [None, None]
    prev_dn_ev = None
    t_acc_fin = {j: t_accs for j in range(NSUB)}
    last_gu_mm = [None, None]
    fb = [
        dict(xqt=xqt[:, :], gsil=gsil, hbuf=hbuf, hqb=hqb, hfm=hfm[:, :],
             bm=qbm[:, 0:48], sc=qsc[:, 0:48], rc=qrc[:, 0:48], f=fslc,
             stg=(pDbf, 2048)),
        dict(xqt=xqt1, gsil=gsil1, hbuf=hbuf1, hqb=hqb1, hfm=hfm1,
             bm=qbm3, sc=qsc3, rc=qrc3, f=f1,
             stg=(pDbf, 2048)),
    ]
    prev_rb = [None, None]
    prev_th = [None, None]

    for si in range(4):
        s0, W = STRIPES[si]
        ncc = W // 128
        ngroups = [(0, 512), (512, W - 512)] if W > 512 else [(0, W)]

        def emit_gu(j, z, gdeps, udeps):
            nonlocal last_gu_mm
            F = fb[z]
            txq = p.op("sync", lambda E, j=j, F=F: E.dma_start(
                out=F["xqt"].rearrange("p (c t) -> p c t", c=8),
                in_=xqfm_d.rearrange("(c p) t -> p c t", p=128)[:, :,
                                     j * 128:(j + 1) * 128]),
                deps=[p.stream_tag("dst_xq")] +
                     ([last_gu_mm[z]] if last_gu_mm[z] else
                      [b_carve_bar] + b_pa_war),
                dma="dxqt" if z == 0 else "dxqt1")
            gmm, umm = [], []
            for (o0, nw) in ngroups:
                for c in range(8):
                    def mkg(c=c, o0=o0, nw=nw, W=W, F=F):
                        return lambda E: E.matmul(
                            pC[:, o0:o0 + nw],
                            F["xqt"][:, c * 128:(c + 1) * 128],
                            wgs[:, c * 768 + o0: c * 768 + o0 + nw],
                            start=(c == 0), stop=(c == 7))
                    dd = ([txq] + t_ws + gdeps) if (c == 0 and o0 == 0) else []
                    gmm.append(p.op("tensor", mkg(), deps=dd))
            for (o0, nw) in ngroups:
                for c in range(8):
                    def mku(c=c, o0=o0, nw=nw, W=W, F=F):
                        return lambda E: E.matmul(
                            pD[:, o0:o0 + nw],
                            F["xqt"][:, c * 128:(c + 1) * 128],
                            wus[:, c * 768 + o0: c * 768 + o0 + nw],
                            start=(c == 0), stop=(c == 7))
                    dd = udeps if (c == 0 and o0 == 0) else []
                    umm.append(p.op("tensor", mku(), deps=dd))
            last_gu_mm[z] = umm[-1]
            return gmm, umm

        guq = {}
        guq[0] = emit_gu(0, 0, [tgs_p],
                         [pdbf_reader[0], pdbf_reader[1]]
                         if si == 0 else ([th_p] if th_p else []))
        for jp in range(NSUB // 2):
            j0, j1 = 2 * jp, 2 * jp + 1
            gmm0, umm0 = guq[j0]
            tgs0 = p.op("scalar", lambda E, W=W, F=fb[0]: E.activation(
                F["gsil"][:, 0:W], pC.ap()[:, 0:W], AF.Silu),
                deps=[gmm0[-1]] + ([prev_th[0]] if prev_th[0] else []))
            th0 = p.op("vector", lambda E, W=W, F=fb[0]: E.tensor_tensor(
                F["hbuf"][:, 0:W], pD.ap()[:, 0:W], F["gsil"][:, 0:W],
                AL.mult),
                deps=[umm0[-1], tgs0] + ([prev_rb[0]] if prev_rb[0] else []))
            guq[j1] = emit_gu(j1, 1, [tgs0], [th0])
            gmm1, umm1 = guq[j1]
            tgs1 = p.op("scalar", lambda E, W=W, F=fb[1]: E.activation(
                F["gsil"][:, 0:W], pC.ap()[:, 0:W], AF.Silu),
                deps=[gmm1[-1]] + ([prev_th[1]] if prev_th[1] else []))
            th1 = p.op("vector", lambda E, W=W, F=fb[1]: E.tensor_tensor(
                F["hbuf"][:, 0:W], pD.ap()[:, 0:W], F["gsil"][:, 0:W],
                AL.mult),
                deps=[umm1[-1], tgs1] + ([prev_rb[1]] if prev_rb[1] else []))
            jobs = []
            for z, thz in ((0, th0), (1, th1)):
                F = fb[z]
                jobs.append(dict(x=F["hbuf"][:, 0:W], q=F["hqb"][:, 0:W],
                                 bm=F["bm"][:, 0:W // 16],
                                 sc=F["sc"][:, 0:W // 16],
                                 rc=F["rc"][:, 0:W // 16],
                                 t1=F["f"][0][:, 0:W], t2=F["f"][1][:, 0:W],
                                 t3=F["f"][2][:, 0:W], F=W, dep=[thz],
                                 final_war=[prev_hfm[z]] if prev_hfm[z]
                                 else []))
            tqhs = emit_quant_multi(p, jobs, mskt[:, 0:1])
            prev_rb = list(p.q_rb_tags)
            if jp + 1 < NSUB // 2:
                guq[2 * jp + 2] = emit_gu(2 * jp + 2, 0,
                                          [tgs1, prev_hfm[1] or tqhs[1]],
                                          [th1])
            thfs = [None, None]
            for z, (tqh, jz) in enumerate(((tqhs[0], j0), (tqhs[1], j1))):
                F = fb[z]
                stgap, stgo = F["stg"]
                mm_t = []
                for cc in range(ncc):
                    def mkt(cc=cc, F=F, stgap=stgap, stgo=stgo):
                        return lambda E: E.transpose(
                            stgap[:, stgo + cc * 128: stgo + (cc + 1) * 128],
                            F["hqb"][:, cc * 128:(cc + 1) * 128], ident[:, :])
                    dd = []
                    if cc == 0:
                        dd = [tqh]
                        for t_ in (prev_hfm[0], prev_hfm[1]):
                            if t_ is not None:
                                dd.append(t_)
                        if th_p is not None:
                            dd.append(th_p)
                            th_p = None
                        if z == 1 and thfs[0] is not None:
                            dd.append(thfs[0])
                    mm_t.append(p.op("tensor", mkt(), deps=dd))
                thf = p.op("scalar", lambda E, ncc=ncc, F=F, stgap=stgap,
                           stgo=stgo: E.activation(
                               F["hfm"][:, 0:ncc * 128],
                               stgap[:, stgo:stgo + ncc * 128], AF.Copy),
                           deps=mm_t)
                prev_hfm[z] = thf
                thfs[z] = thf
            for z, jz in ((0, j0), (1, j1)):
                F = fb[z]
                dmm = []
                for ng in range(2):
                    for cc in range(ncc):
                        def mkdn(cc=cc, ng=ng, ncc=ncc, F=F):
                            return lambda E: E.matmul(
                                pA[:, ng * 512:(ng + 1) * 512],
                                F["hfm"][:, cc * 128:(cc + 1) * 128],
                                wds[:, cc * 1024 + ng * 512:
                                     cc * 1024 + (ng + 1) * 512],
                                start=(cc == 0), stop=(cc == ncc - 1))
                        dd = []
                        if cc == 0 and ng == 0:
                            dd = [thfs[z]]
                            if prev_dn_ev is not None:
                                dd.append(prev_dn_ev)
                            else:
                                dd += b_pa_war
                        dmm.append(p.op("tensor", mkdn(), deps=dd))
                tdn = p.op("vector", lambda E, j=jz: E.scalar_tensor_tensor(
                    acc[:, j * D:(j + 1) * D], pA.ap(), 1.0,
                    acc[:, j * D:(j + 1) * D], AL.mult, AL.add),
                    deps=[dmm[-1], t_acc_fin[jz]])
                prev_dn_ev = tdn
                t_acc_fin[jz] = tdn
            prev_th = [th0, th1]
        tgs_p = tgs1
        th_p = th1
        if si + 1 < 4:
            t_ws = load_stripe(si + 1, [p.barrier_tag("tensor")])
            guq[0] = emit_gu(0, 0, [tgs_p, prev_hfm[1] or tgs_p],
                             [th_p] if th_p else [])

    b_last_tgs = tgs_p
    b_last_th = th_p
    b_quant_done = [prev_hfm[0], prev_hfm[1], prev_dn_ev]

    t_cini = p.op("vector", lambda E: E.memset(cstA[:, :], 0.0))
    t_off = t_cini
    for r in range(8):
        def mkld(r=r):
            return lambda E: E.dma_start(
                out=cstB[:, :], in_=cgall.ap()[r * 128:(r + 1) * 128, :])
        tl = p.op("sync", mkld(), deps=[t_ccc, t_off, t_cst], dma="dcg")
        def mkacc(r=r):
            return lambda E: E.scalar_tensor_tensor(
                cstA[:, :], cstB[:, :], mko[:, r:r + 1], cstA[:, :],
                AL.mult, AL.add)
        t_off = p.op("vector", mkacc(), deps=[tl, t_mask, t_off])

    if stop_after == "C":
        if dbg:
            p.op("sync", lambda E: E.dma_start(out=dbg_cst.ap(),
                                               in_=cstA[:, :]),
                 deps=[t_off], dma="dcg")
            p.op("sync", lambda E: E.dma_start(out=dbg_mk.ap(),
                                               in_=mko[:, :]),
                 deps=[t_mask], dma="dcg")
        return finish_truncated()

    t_wq = []
    tens_bar = p.barrier_tag("tensor")
    prev_outp = prev_dn_ev
    t_prev_cadd = b_last_tgs
    t_prev_attn_evac = [b_last_th, prev_hfm[0], prev_hfm[1]]
    for c in range(8):
        def mk(c=c):
            return lambda E: E.dma_start(
                out=wbig[:, c * 1024:(c + 1) * 1024],
                in_=wqkvt[c * 128:(c + 1) * 128, 0:D])
        t_wq.append(p.op("gpsimd", mk(), dma="dgp",
                         deps=([tens_bar] + b_quant_done) if c == 0
                         else []))
    for c in range(8):
        def mk(c=c):
            return lambda E: E.dma_start(
                out=wbig[:, 8192 + c * 1024: 8192 + (c + 1) * 1024],
                in_=woutt[c * 128:(c + 1) * 128, :])
        t_wq.append(p.op("gpsimd", mk(), dma="dgp"))

    cst = [cstA, cstB]
    t_cprev = t_off
    acc_outp = {}
    prev_u = None
    prev_afm_ev = None
    t_acc_load = {}
    prev_kq = None
    prev_vq = None

    for q in range(NQT):
        i0 = q * 2
        tqx = p.op("sync", lambda E, q=q: E.dma_start(
            out=xnq[:, :].rearrange("p (c t) -> p c t", c=8),
            in_=xnfm_d.rearrange("(c p) t -> p c t", p=128)[:, :,
                                 q * QTT:(q + 1) * QTT]),
            deps=[p.stream_tag("dst_xn")] +
                 ([prev_u] if prev_u else []), dma="dqx")
        t_uex = [None] * 8
        for m in range(8):
            half = m % 2
            mm_tags = []
            for c in range(8):
                def mkm(m=m, c=c, half=half):
                    return lambda E: E.matmul(
                        pA[:, half * 512: half * 512 + QTT],
                        wbig[:, c * 1024 + m * 128: c * 1024 + (m + 1) * 128],
                        xnq[:, c * QTT:(c + 1) * QTT],
                        start=(c == 0), stop=(c == 7))
                dd = []
                if c == 0:
                    dd = [tqx]
                    if m == 0 and q == 0:
                        dd += t_wq
                        dd.append(half_evac[0])
                        dd.append(half_evac[1])
                    if m >= 2:
                        dd.append(t_uex[m - 2])
                    if m < 2 and prev_outp is not None:
                        dd.append(prev_outp)
                mm_tags.append(p.op("tensor", mkm(), deps=dd))
            t_uex[m] = p.op("scalar", lambda E, m=m, half=half: E.activation(
                uq[:, m * QTT:(m + 1) * QTT],
                pA[:, half * 512: half * 512 + QTT], AF.Exp),
                deps=[mm_tags[-1]] + ([t_cst] if q == 0 else []))
        prev_u = t_uex[-1]
        if stop_after == "A0":
            continue
        for s in range(2):
            i = i0 + s
            tkq = p.op("sync", lambda E, i=i: E.dma_start(
                out=kq[:, :], in_=k_d[i * 128:(i + 1) * 128, :]),
                deps=[p.stream_tag("dst_k")] +
                     ([prev_kq] if prev_kq else []),
                dma="dkq")
            tvq = p.op("sync", lambda E, i=i: E.dma_start(
                out=vq[:, :], in_=v_d[i * 128:(i + 1) * 128, :]),
                deps=[p.stream_tag("dst_v")] +
                     ([prev_vq] if prev_vq else []),
                dma="dvq")
            for b in range(2):
                n = s * 2 + b
                g = q * 4 + n
                cur = cst[g % 2]
                nxt = cst[(g + 1) % 2]
                attn_mm = []
                horder = [hh for hh in range(NH) if hh % 2 == 0] + \
                         [hh for hh in range(NH) if hh % 2 == 1]
                for h in horder:
                    pr = h // 2
                    par = h % 2
                    def mka_(h=h, pr=pr, par=par, n=n, cur=cur):
                        return lambda E: E.matmul(
                            pD[0:64, head_off(h):head_off(h) + 65],
                            uq[par * 64:par * 64 + 64,
                               pr * QTT + n * 64: pr * QTT + (n + 1) * 64],
                            cur[par * 64:par * 64 + 64,
                                pair_off(pr) + par * 65:
                                pair_off(pr) + par * 65 + 65],
                            start=True, stop=True)
                    dd = []
                    if h == 0:
                        dd = [t_cprev, t_uex[7]] + t_prev_attn_evac
                    attn_mm.append(p.op("tensor", mka_(), deps=dd))
                cd_mm = []
                for pr in range(NPAIR):
                    def mkc_(pr=pr, b=b):
                        return lambda E: E.matmul(
                            pC[:, pair_off(pr):pair_off(pr) + 130],
                            kq[b * 64:(b + 1) * 64,
                               pr * 128:(pr + 1) * 128],
                            vq[b * 64:(b + 1) * 64,
                               pr * 130:(pr + 1) * 130],
                            start=True, stop=True)
                    dd = []
                    if pr == 0:
                        dd = [tkq, tvq, t_prev_cadd]
                    cd_mm.append(p.op("tensor", mkc_(), deps=dd))
                t_cprev = p.op(
                    "vector",
                    lambda E, cur=cur, nxt=nxt: E.scalar_tensor_tensor(
                        nxt[:, :], pC.ap(), 1.0, cur[:, :], AL.mult, AL.add),
                    deps=[cd_mm[-1], t_cprev])
                t_prev_cadd = t_cprev
                tden = []
                for gi, (h0, hn) in enumerate(HEAD_GROUPS):
                    def mkd_(gi=gi, h0=h0, hn=hn):
                        base = 512 * gi
                        return lambda E: E.tensor_scalar(
                            dn[:, h0:h0 + hn],
                            pD.ap()[0:64, base:base + 65 * hn].rearrange(
                                "p (h c) -> p h c", c=65)[:, :,
                                                          64:65].squeeze(2),
                            1e-30, None, AL.max)
                    tden.append(p.op("vector", mkd_(), deps=[attn_mm[-1]]))
                trec = p.op("vector", lambda E: E.reciprocal(dq[:, :],
                                                             dn[:, :]),
                            deps=tden)
                ev_tags = []
                for gi, (h0, hn) in enumerate(HEAD_GROUPS):
                    def mke_(gi=gi, h0=h0, hn=hn, b=b):
                        base = 512 * gi
                        return lambda E: E.tensor_tensor(
                            atm[b * 64:(b + 1) * 64,
                                h0 * 64:(h0 + hn) * 64].rearrange(
                                    "p (h e) -> p h e", e=64),
                            pD.ap()[0:64, base:base + 65 * hn].rearrange(
                                "p (h c) -> p h c", c=65)[:, :, 0:64],
                            dq[:, h0:h0 + hn].unsqueeze(2).broadcast_to(
                                [64, hn, 64]), AL.mult)
                    ev_tags.append(p.op("vector", mke_(), deps=[trec]))
                t_prev_attn_evac = ev_tags
            prev_kq = t_prev_cadd
            prev_vq = t_prev_cadd
            if stop_after == "A1":
                continue
            mm_t = []
            for c in range(8):
                def mkt_(c=c):
                    return lambda E: E.transpose(
                        pAbf[:, c * 128:(c + 1) * 128],
                        atm[:, c * 128:(c + 1) * 128],
                        ident[:, :])
                dd = []
                if c == 0:
                    dd = list(t_prev_attn_evac)
                    if prev_afm_ev is not None:
                        dd.append(prev_afm_ev)
                    if prev_outp is not None:
                        dd.append(prev_outp)
                mm_t.append(p.op("tensor", mkt_(), deps=dd))
            tafm = p.op("vector", lambda E: E.tensor_copy(afm[:, :],
                                                          pAbf[:, 0:1024]),
                        deps=mm_t)
            prev_afm_ev = tafm
            mm_o = []
            for ng in range(2):
                for c in range(8):
                    def mko_(c=c, ng=ng):
                        return lambda E: E.matmul(
                            pA[:, ng * 512:(ng + 1) * 512],
                            afm[:, c * 128:(c + 1) * 128],
                            wbig[:, 8192 + c * 1024 + ng * 512:
                                 8192 + c * 1024 + (ng + 1) * 512],
                            start=(c == 0), stop=(c == 7))
                    dd = [tafm] if (c == 0 and ng == 0) else []
                    mm_o.append(p.op("tensor", mko_(), deps=dd))
            t_out = p.op("vector", lambda E, i=i: E.scalar_tensor_tensor(
                acc[:, i * D:(i + 1) * D], pA.ap(), 1.0,
                acc[:, i * D:(i + 1) * D], AL.mult, AL.add),
                deps=[mm_o[-1], t_acc_fin[i]])
            acc_outp[i] = t_out
            prev_outp = t_out

    if stop_after in ("A0", "A1"):
        return finish_truncated()
    if stop_after == "A":
        sta = []
        for i in range(NSUB):
            def mksa(i=i):
                return lambda E: E.dma_start(
                    out=out[i * 128:(i + 1) * 128, :],
                    in_=acc[:, i * D:(i + 1) * D])
            sta.append(p.op("sync", mksa(), deps=[acc_outp[i]],
                            dma="dout"))
        p.op("sync", lambda E: E.nop(), deps=sta, inc=False)
        return finish_truncated()

    st_tags = []
    for i in range(NSUB):
        def mks(i=i):
            return lambda E: E.dma_start(out=out[i * 128:(i + 1) * 128, :],
                                         in_=acc[:, i * D:(i + 1) * D])
        st_tags.append(p.op("sync", mks(), deps=[acc_outp[i]],
                                dma="dout"))
    p.op("sync", lambda E: E.nop(), deps=st_tags, inc=False)

    p.emit(block, sems)
    ctx.close()
    return nc


_NC_CACHE = None


def kernel(L0_attn_x, L0_attn_W_qkv, L0_attn_W_out, L0_ffn_w_gate,
           L0_ffn_w_up, L0_ffn_w_down):
    global _NC_CACHE
    x = np.asarray(L0_attn_x, dtype=np.float32)
    wqkv = np.asarray(L0_attn_W_qkv, dtype=np.float32)
    wout = np.asarray(L0_attn_W_out, dtype=np.float32)
    wg = np.asarray(L0_ffn_w_gate, dtype=np.float32)
    wu = np.asarray(L0_ffn_w_up, dtype=np.float32)
    wd = np.asarray(L0_ffn_w_down, dtype=np.float32)

    wqkvt = np.ascontiguousarray(wqkv.T)
    woutt = np.ascontiguousarray(wout.T)

    in_maps = []
    for c in range(8):
        b, t0 = c // 4, (c % 4) * TC
        m = np.zeros(8, dtype=np.float32)
        m[(c // 4) * 4:c] = 1.0
        in_maps.append({
            "x_c": np.ascontiguousarray(x[b, t0:t0 + TC, :]),
            "wqkvt": wqkvt,
            "woutt": woutt,
            "wg_sh": np.ascontiguousarray(wg[c * WSH:(c + 1) * WSH, :]),
            "wu_sh": np.ascontiguousarray(wu[c * WSH:(c + 1) * WSH, :]),
            "wd_sh": np.ascontiguousarray(wd[c * DSH:(c + 1) * DSH, :]),
            "cmask": m,
            "crank": np.array([[c]], dtype=np.int32),
        })

    if _NC_CACHE is None:
        _NC_CACHE = build_core()
    res = run_bass_kernel_spmd(_NC_CACHE, in_maps, core_ids=list(range(8)))

    outp = np.empty((2, 8192, D), dtype=np.float32)
    for c in range(8):
        b, t0 = c // 4, (c % 4) * TC
        outp[b, t0:t0 + TC, :] = res.results[c]["out_c"]
    return outp


# version 27
# speedup vs baseline: 1.0248x; 1.0178x over previous
import numpy as np
import concourse.bass as bass
import concourse.mybir as mybir
from concourse.bass_utils import run_bass_kernel_spmd

F32 = mybir.dt.float32
BF16 = mybir.dt.bfloat16
I32 = mybir.dt.int32
AL = mybir.AluOpType
AF = mybir.ActivationFunctionType
AX = mybir.AxisListType.X

D = 1024
TC = 2048
NSUB = 16
NQT = 8
QTT = 256
NH = 16
HD = 64
NPAIR = 8
II = 2816
WSH = II // 8
DSH = D // 8
SHE = 1024 * WSH
GINE = 3 * SHE
STRIPES = [(0, 768), (768, 768), (1536, 640), (2176, 640)]


def head_off(h):
    return 512 * (h // 7) + 65 * (h % 7)


def pair_off(p):
    return 512 * (p // 3) + 130 * (p % 3)


HEAD_GROUPS = [(0, 7), (7, 7), (14, 2)]

MAGIC_H = float(np.float32(1.5 * 2**22))
ROUND_BIT = 0x00200000
MANT_MASK = -0x00400000

ENGS = ["sync", "gpsimd", "vector", "scalar", "tensor"]
DMA_STREAMS = ["dck", "dst_xn", "dst_xq", "dst_k", "dst_v", "dgw", "dcg",
               "dmask", "dcrk", "dqx", "dacc", "dkq", "dvq", "dxqt", "dxqt1",
               "dstrip",
               "dout", "dwb", "dgp", "dgo"]
SEMS = ENGS + ["cc0", "cc1"] + DMA_STREAMS


class P:

    def __init__(self):
        self.items = {e: [] for e in ENGS}
        self.cnt = {e: 0 for e in ENGS}
        self.opidx = {e: 0 for e in ENGS}
        self.drained = {e: 0 for e in ENGS}
        self.waited = {e: {f: 0 for f in SEMS} for e in ENGS}
        self.scnt = {s: 0 for s in SEMS}

    def dep(self, e, d):
        if d is None:
            return
        de, didx, dval, ddma = d
        if de not in ENGS:
            if dval > self.waited[e][de]:
                self.items[e].append(("wait", (de, dval), []))
                self.waited[e][de] = dval
            return
        if de == e and not ddma:
            if didx > self.drained[e]:
                self.items[e].append(("drain", None, []))
                self.drained[e] = self.opidx[e]
        else:
            if dval > self.waited[e][de]:
                self.items[e].append(("wait", (de, dval), []))
                self.waited[e][de] = dval

    def op(self, e, fn, deps=(), dma=False, inc=True):
        best = {}
        need_drain = False
        drain_idx = 0
        for d in deps:
            if d is None:
                continue
            de, didx, dval, ddma = d
            if de == e and ddma is False and de in ENGS:
                need_drain = True
                drain_idx = max(drain_idx, didx)
            else:
                if dval > best.get(de, 0):
                    best[de] = dval
        for de, dval in best.items():
            self.dep(e, (de, 0, dval, True))
        if need_drain:
            self.dep(e, (e, drain_idx, 0, False))
        self.opidx[e] += 1
        idx = self.opidx[e]
        incs = []
        if isinstance(dma, str):
            self.scnt[dma] += 16
            incs = [(dma, 16)]
            self.items[e].append(("op", fn, incs))
            return (dma, idx, self.scnt[dma], True)
        if inc or dma:
            n = 16 if dma else 1
            self.cnt[e] += n
            incs = [(e, n)]
        self.items[e].append(("op", fn, incs))
        return (e, idx, self.cnt[e], dma)

    def barrier_tag(self, e):
        return (e, self.opidx[e], self.cnt[e], False)

    def stream_tag(self, s):
        return (s, 0, self.scnt[s], True)

    def emit(self, block, sems):
        def run(e, eng):
            for kind, a, incs in self.items[e]:
                if kind == "drain":
                    eng.drain()
                elif kind == "wait":
                    eng.wait_ge(sems[a[0]], a[1])
                else:
                    ins = a(eng)
                    for (se, n) in incs:
                        ins.then_inc(sems[se], n)

        block.sync(lambda eng: run("sync", eng))
        block.gpsimd(lambda eng: run("gpsimd", eng))
        block.vector(lambda eng: run("vector", eng))
        block.scalar(lambda eng: run("scalar", eng))
        block.tensor(lambda eng: run("tensor", eng))


def emit_quant(p, eng, x_ap, q_ap, bm, sc, rc, t1, t2, t3, F, dep,
               final_war=(), mbc=None):
    NB = F // 16
    xv = lambda ap: ap.rearrange("p (n b) -> p n b", b=16)
    V, G = "vector", "gpsimd"
    dep = list(dep)
    mbcb = mbc.unsqueeze(2).broadcast_to([128, 1, F]).squeeze(1)
    tr = p.op(V, lambda E: E.reduce_max(bm, xv(x_ap), axis=AX,
                                        apply_absolute_value=True), deps=dep)
    ts = p.op(G, lambda E: E.tensor_scalar(sc, bm, 6e-12, float(1.0 / 6.0),
                                           AL.max, AL.mult), deps=[tr])
    tc = p.op(V, lambda E: E.reciprocal(rc, sc), deps=[ts])
    rcb = rc.unsqueeze(2).broadcast_to([128, NB, 16])
    scb = sc.unsqueeze(2).broadcast_to([128, NB, 16])
    tb = p.op(G, lambda E: E.tensor_tensor(xv(t2), xv(x_ap), rcb, AL.mult),
              deps=[tc] + dep)
    p.q_rb_tag = tb
    rb = t2
    tk = p.op(V, lambda E: E.tensor_scalar(t1, rb, MAGIC_H, MAGIC_H,
                                           AL.add, AL.subtract), deps=[tb])
    tp_ = p.op(V, lambda E: E.tensor_scalar(t3.bitcast(I32), rb.bitcast(I32),
                                            ROUND_BIT, None, AL.add),
               deps=[tb])
    tq1 = p.op(V, lambda E: E.tensor_scalar(t3.bitcast(I32), t3.bitcast(I32),
                                            MANT_MASK, None, AL.bitwise_and),
               deps=[tp_])
    tzc = p.op(V, lambda E: E.tensor_scalar(t2, t3, -1.0, 1.0, AL.max, AL.min),
               deps=[tq1, tk, tp_])
    th_ = p.op(V, lambda E: E.tensor_scalar(t1, t1, -1.0, 1.0, AL.max, AL.min),
               deps=[tk])
    tm = p.op(G, lambda E: E.tensor_tensor(t2, t1, t2, AL.subtract),
              deps=[th_, tzc])
    tl = p.op(V, lambda E: E.tensor_tensor(t3, t2, t3, AL.add),
              deps=[tm, tq1])
    tf = p.op(G, lambda E: E.tensor_tensor(xv(q_ap), xv(t3), scb, AL.mult),
              deps=[tl] + list(final_war))
    return tf


def emit_quant_multi(p, jobs, mbc):
    V, G = "vector", "gpsimd"
    n = len(jobs)
    st = [dict() for _ in range(n)]
    xvs = []
    for j in jobs:
        F = j["F"]
        j["NB"] = F // 16
        xvs.append(None)

    def xv(ap):
        return ap.rearrange("p (n b) -> p n b", b=16)

    for idx, j in enumerate(jobs):
        st[idx]["r"] = p.op(V, (lambda j: lambda E: E.reduce_max(
            j["bm"], xv(j["x"]), axis=AX, apply_absolute_value=True))(j),
            deps=list(j["dep"]))
    for idx, j in enumerate(jobs):
        st[idx]["s"] = p.op(G, (lambda j: lambda E: E.tensor_scalar(
            j["sc"], j["bm"], 6e-12, float(1.0 / 6.0), AL.max, AL.mult))(j),
            deps=[st[idx]["r"]])
    for idx, j in enumerate(jobs):
        st[idx]["c"] = p.op(V, (lambda j: lambda E: E.reciprocal(
            j["rc"], j["sc"]))(j), deps=[st[idx]["s"]])
    for idx, j in enumerate(jobs):
        NB = j["NB"]
        rcb = j["rc"].unsqueeze(2).broadcast_to([128, NB, 16])
        st[idx]["b"] = p.op(G, (lambda j, rcb: lambda E: E.tensor_tensor(
            xv(j["t2"]), xv(j["x"]), rcb, AL.mult))(j, rcb),
            deps=[st[idx]["c"]] + list(j["dep"]))
        st[idx]["b_tag"] = st[idx]["b"]
    for idx, j in enumerate(jobs):
        st[idx]["k"] = p.op(V, (lambda j: lambda E: E.tensor_scalar(
            j["t1"], j["t2"], MAGIC_H, MAGIC_H, AL.add, AL.subtract))(j),
            deps=[st[idx]["b"]])
    for idx, j in enumerate(jobs):
        st[idx]["p"] = p.op(V, (lambda j: lambda E: E.tensor_scalar(
            j["t3"].bitcast(I32), j["t2"].bitcast(I32), ROUND_BIT, None,
            AL.add))(j), deps=[st[idx]["b"]])
    for idx, j in enumerate(jobs):
        st[idx]["q1"] = p.op(V, (lambda j: lambda E: E.tensor_scalar(
            j["t3"].bitcast(I32), j["t3"].bitcast(I32), MANT_MASK, None,
            AL.bitwise_and))(j), deps=[st[idx]["p"]])
    for idx, j in enumerate(jobs):
        st[idx]["zc"] = p.op(V, (lambda j: lambda E: E.tensor_scalar(
            j["t2"], j["t3"], -1.0, 1.0, AL.max, AL.min))(j),
            deps=[st[idx]["q1"], st[idx]["k"]])
    for idx, j in enumerate(jobs):
        st[idx]["hcl"] = p.op(V, (lambda j: lambda E: E.tensor_scalar(
            j["t1"], j["t1"], -1.0, 1.0, AL.max, AL.min))(j),
            deps=[st[idx]["k"]])
    for idx, j in enumerate(jobs):
        st[idx]["m"] = p.op(G, (lambda j: lambda E: E.tensor_tensor(
            j["t2"], j["t1"], j["t2"], AL.subtract))(j),
            deps=[st[idx]["hcl"], st[idx]["zc"]])
    for idx, j in enumerate(jobs):
        st[idx]["l"] = p.op(V, (lambda j: lambda E: E.tensor_tensor(
            j["t3"], j["t2"], j["t3"], AL.add))(j),
            deps=[st[idx]["m"], st[idx]["q1"]])
    outs = []
    for idx, j in enumerate(jobs):
        NB = j["NB"]
        scb = j["sc"].unsqueeze(2).broadcast_to([128, NB, 16])
        outs.append(p.op(G, (lambda j, scb: lambda E: E.tensor_tensor(
            xv(j["q"]), xv(j["t3"]), scb, AL.mult))(j, scb),
            deps=[st[idx]["l"]] + list(j["final_war"])))
    p.q_rb_tags = [st[i]["b_tag"] for i in range(n)]
    return outs


def build_core(stop_after=None, dbg=False):
    nc = bass.Bass("TRN2", target_bir_lowering=False)
    dkind = "ExternalOutput" if dbg else "Internal"

    x_in = nc.dram_tensor("x_c", [TC, D], F32, kind="ExternalInput").ap()
    wqkvt = nc.dram_tensor("wqkvt", [D, 3 * D], F32, kind="ExternalInput").ap()
    woutt = nc.dram_tensor("woutt", [D, D], F32, kind="ExternalInput").ap()
    wg_sh = nc.dram_tensor("wg_sh", [WSH, D], F32, kind="ExternalInput").ap()
    wu_sh = nc.dram_tensor("wu_sh", [WSH, D], F32, kind="ExternalInput").ap()
    wd_sh = nc.dram_tensor("wd_sh", [DSH, II], F32, kind="ExternalInput").ap()
    cmask = nc.dram_tensor("cmask", [8], F32, kind="ExternalInput").ap()
    crank = nc.dram_tensor("crank", [1, 1], I32, kind="ExternalInput").ap()
    out = nc.dram_tensor("out_c", [TC, D], F32, kind="ExternalOutput").ap()

    xnfm_d = nc.dram_tensor("xnfm_d", [D, TC], BF16, kind=dkind).ap()
    xqfm_d = nc.dram_tensor("xqfm_d", [D, TC], BF16, kind=dkind).ap()
    k_d = nc.dram_tensor("k_d", [TC, D], BF16, kind=dkind).ap()
    v_d = nc.dram_tensor("v_d", [TC, NH * 65], BF16, kind=dkind).ap()
    wgin = nc.dram_tensor("wgin", [GINE], BF16)
    wgall = nc.dram_tensor("wgall", [8 * GINE], BF16, addr_space="Shared")
    cgin = nc.dram_tensor("cgin", [128, 1536], F32)
    cgall = nc.dram_tensor("cgall", [8 * 128, 1536], F32, addr_space="Shared")
    if dbg:
        dbg_wg = nc.dram_tensor("dbg_wg", [8 * GINE], BF16,
                                kind="ExternalOutput")
        dbg_cg = nc.dram_tensor("dbg_cg", [8 * 128, 1536], F32,
                                kind="ExternalOutput")
        dbg_cst = nc.dram_tensor("dbg_cst", [128, 1536], F32,
                                 kind="ExternalOutput")
        dbg_mk = nc.dram_tensor("dbg_mk", [128, 8], F32,
                                kind="ExternalOutput")

    from contextlib import ExitStack
    ctx = ExitStack()
    e = ctx.enter_context

    acc = e(nc.sbuf_tensor("acc", [128, NSUB * D], F32))
    cstA = e(nc.sbuf_tensor("cstA", [128, 1536], F32))
    cstB = e(nc.sbuf_tensor("cstB", [128, 1536], F32))
    ident = e(nc.sbuf_tensor("ident", [128, 128], BF16))
    mko = e(nc.sbuf_tensor("mko", [128, 8], F32))

    arf = e(nc.sbuf_tensor("arf", [128, 6144], F32))
    arb = e(nc.sbuf_tensor("arb", [128, 3072], BF16))
    tev = e(nc.sbuf_tensor("tev", [128, 1024], BF16))
    wbig = e(nc.sbuf_tensor("wbig", [128, 8 * 2048], BF16))
    wgs = e(nc.sbuf_tensor("wgs", [128, 8 * 768], BF16))
    wus = e(nc.sbuf_tensor("wus", [128, 8 * 768], BF16))
    wds = e(nc.sbuf_tensor("wds", [128, 6 * 1024], BF16))

    qbm = e(nc.sbuf_tensor("qbm", [128, 64], F32))
    qsc = e(nc.sbuf_tensor("qsc", [128, 64], F32))
    qrc = e(nc.sbuf_tensor("qrc", [128, 64], F32))

    ks = e(nc.sbuf_tensor("ks", [128, D], BF16))
    vs = e(nc.sbuf_tensor("vs", [128, NH * 65], BF16))
    st1 = e(nc.sbuf_tensor("st1", [128, 1], F32))
    st2 = e(nc.sbuf_tensor("st2", [128, 1], F32))
    st3 = e(nc.sbuf_tensor("st3", [128, 1], F32))
    st4 = e(nc.sbuf_tensor("st4", [128, 1], F32))
    st1b = e(nc.sbuf_tensor("st1b", [128, 1], F32))
    st2b = e(nc.sbuf_tensor("st2b", [128, 1], F32))
    st3b = e(nc.sbuf_tensor("st3b", [128, 1], F32))
    st4b = e(nc.sbuf_tensor("st4b", [128, 1], F32))
    qbm2 = e(nc.sbuf_tensor("qbm2", [128, 64], F32))
    qsc2 = e(nc.sbuf_tensor("qsc2", [128, 64], F32))
    qrc2 = e(nc.sbuf_tensor("qrc2", [128, 64], F32))
    epsv = e(nc.sbuf_tensor("epsv", [128, 1], F32))
    mskt = e(nc.sbuf_tensor("mskt", [128, 1], I32))
    crk = e(nc.sbuf_tensor("crk", [1, 1], I32))
    rg_base = nc.sync.alloc_register("rgbase")
    rg_tmp = nc.sync.alloc_register("rgtmp")

    xnq = e(nc.sbuf_tensor("xnq", [128, 8 * QTT], BF16))
    uq = e(nc.sbuf_tensor("uq", [128, 8 * QTT], F32))
    kq = e(nc.sbuf_tensor("kq", [128, D], BF16))
    vq = e(nc.sbuf_tensor("vq", [128, NH * 65], BF16))
    atm = e(nc.sbuf_tensor("atm", [128, D], BF16))
    afm = e(nc.sbuf_tensor("afm", [128, 8 * 128], BF16))
    dq = e(nc.sbuf_tensor("dq", [64, 16], F32))
    dn = e(nc.sbuf_tensor("dn", [64, 16], F32))

    xqt = e(nc.sbuf_tensor("xqt", [128, 8 * 128], BF16))
    hfm = e(nc.sbuf_tensor("hfm", [128, 6 * 128], BF16))

    iwork = arf[:, 0:128].bitcast(I32)
    wbuf = arf[:, 0:3072]
    qslc = [arf[:, 3072:4096], arf[:, 4096:5120], arf[:, 5120:6144]]
    xs = arf[:, 0:1024]
    xn = arf[:, 1024:2048]
    hbuf = arf[:, 0:768]
    fslc = [arf[:, 3072 + 1024 * i: 3072 + 1024 * i + 768] for i in range(3)]
    wq_b = arb[:, 0:3072]
    xnb = arb[:, 0:1024]
    xqb = arb[:, 1024:2048]
    gsil = arb[:, 0:768]
    hqb = arb[:, 1024:1792]
    wgs_f = wgs.ap().bitcast(F32)
    wus_f = wus.ap().bitcast(F32)
    xs1 = wgs_f[:, 0:1024]
    xn1 = wgs_f[:, 1024:2048]
    qA1 = wgs_f[:, 2048:3072]
    qB1 = wus_f[:, 0:1024]
    qC1 = wus_f[:, 1024:2048]
    wbig_f = wbig.ap().bitcast(F32)
    hbuf1 = arf[:, 768:1536]
    f1 = [wbig_f[:, 4224:4992], wbig_f[:, 4992:5760], wbig_f[:, 5760:6528]]
    qbm3 = arf[:, 1536:1584]
    qsc3 = arf[:, 1584:1632]
    qrc3 = arf[:, 1632:1680]
    wbig_b = wbig[:, :]
    gsil1 = wbig_b[:, 13056:13824]
    hqb1 = wbig_b[:, 13824:14592]
    hfm1 = wbig_b[:, 14592:15360]
    xqt1 = wbig_b[:, 15360:16384]
    xnb1 = wds[:, 0:1024]
    xqb1 = wds[:, 1024:2048]
    tev1b = wds[:, 2048:3072]
    ks1 = wds[:, 3072:4096]
    vs1 = wds[:, 4096:5136]

    pA = e(nc.psum_tensor("pA", [128, 1024], F32))
    pC = e(nc.psum_tensor("pC", [128, 1536], F32))
    pD = e(nc.psum_tensor("pD", [128, 1536], F32))
    pAbf = pA.ap().bitcast(BF16)
    pCbf = pC.ap().bitcast(BF16)
    pDbf = pD.ap().bitcast(BF16)

    sems = {}
    for name in SEMS:
        sems[name] = e(nc.semaphore("s_" + name))
    block = e(nc.Block())

    p = P()

    t_id0 = p.op("gpsimd", lambda E: E.iota(iwork, pattern=[[1, 128]],
                                            base=0, channel_multiplier=-1))
    t_id1 = p.op("gpsimd", lambda E: E.tensor_scalar(iwork, iwork,
                                                     0, None, AL.is_equal),
                 deps=[t_id0])
    t_ident = p.op("gpsimd", lambda E: E.tensor_copy(ident[:, :], iwork),
                   deps=[t_id1])
    t_mask = p.op("sync", lambda E: E.dma_start(
        out=mko[:, :], in_=cmask.unsqueeze(0).broadcast_to([128, 8])),
        dma="dmask")
    t_vones = p.op("vector", lambda E: E.memset(
        vs[:, :].rearrange("p (h c) -> p h c", c=65)[:, :, 64:65], 1.0))
    t_eps = p.op("vector", lambda E: E.memset(epsv[:, :], 1e-5))
    t_msk = p.op("vector", lambda E: E.memset(mskt[:, :], MANT_MASK))
    t_crk = p.op("sync", lambda E: E.dma_start(out=crk[:, :], in_=crank),
                 dma="dcrk")
    def _ldrank(E):
        ins = E.reg_load(rg_base, crk[0:1, 0:1])
        E.reg_mul(rg_base, rg_base, GINE)
        return ins
    p.op("sync", _ldrank, deps=[t_crk], inc=False)
    t_pzA = p.op("vector", lambda E: E.memset(pA.ap(), 0.0))
    t_pzC = p.op("vector", lambda E: E.memset(pC.ap(), 0.0))
    t_pzD = p.op("vector", lambda E: E.memset(pD.ap(), 0.0))

    wshard_specs = [
        (wg_sh, [128, 128, 96], D, 0, WSH),
        (wu_sh, [128, 128, 96], D, SHE, WSH),
        (wd_sh, [128], II, 2 * SHE, DSH),
    ]
    t_wgin_writes = []
    prev_tq = None
    prev_last_mm = None
    w_last_tv = None
    for din, rowtiles, rowlen, gbase, grs in wshard_specs:
        n_rt = len(rowtiles)
        ld_tags = []
        for rt, nr in enumerate(rowtiles):
            def mkld(rt=rt, nr=nr, din=din, rowlen=rowlen):
                return lambda E: E.dma_start(
                    out=wbuf[0:nr, rt * rowlen:(rt + 1) * rowlen]
                    if rowlen == D else wbuf[0:nr, 0:rowlen],
                    in_=din[rt * 128: rt * 128 + nr, :])
            ld_tags.append(p.op("sync", mkld(),
                                deps=([prev_tq] if rt == 0 else []) +
                                     ([t_ident] if prev_tq is None else []),
                                dma="dwb"))
        if rowtiles[-1] < 128:
            ld_tags.append(p.op("vector", lambda E, n_rt=n_rt, rowlen=rowlen:
                                E.memset(wbuf[96:128, (n_rt - 1) * rowlen:
                                              n_rt * rowlen], 0.0),
                                deps=[prev_tq] if prev_tq else []))
        Ftot = n_rt * rowlen
        slc_bounds = []
        o = 0
        while o < Ftot:
            w = min(1024, Ftot - o)
            slc_bounds.append((o, w))
            o += w
        tq = None
        for (o, w) in slc_bounds:
            tq = emit_quant(p, "vector", wbuf[:, o:o + w] if Ftot == 3072 else
                            arf[:, o:o + w],
                            wq_b[:, o:o + w], qbm[:, 0:w // 16],
                            qsc[:, 0:w // 16], qrc[:, 0:w // 16],
                            qslc[0][:, 0:w], qslc[1][:, 0:w], qslc[2][:, 0:w],
                            w, (ld_tags + [t_msk]) if o == 0 else [tq],
                            final_war=[prev_last_mm] if (o == 0 and
                                                         prev_last_mm) else [],
                            mbc=mskt[:, 0:1])
        prev_tq = tq
        ncc = rowlen // 128
        for rt, nr in enumerate(rowtiles):
            for g0 in range(0, ncc, 8):
                gn = min(8, ncc - g0)
                mm_tags = []
                for j in range(gn):
                    cc = g0 + j
                    def mkt(rt=rt, cc=cc, j=j, rowlen=rowlen):
                        return lambda E: E.transpose(
                            pDbf[:, j * 128:(j + 1) * 128],
                            wq_b[:, rt * rowlen + cc * 128:
                                 rt * rowlen + (cc + 1) * 128],
                            ident[:, :])
                    dd = []
                    if j == 0:
                        dd = [tq, t_ident, t_pzD] + t_wgin_writes[-1:]
                    mm_tags.append(p.op("tensor", mkt(), deps=dd))
                prev_last_mm = mm_tags[-1]
                tv = p.op("scalar", lambda E, gn=gn: E.activation(
                    tev[:, 0:gn * 128], pDbf[:, 0:gn * 128], AF.Copy),
                    deps=mm_tags)
                w_last_tv = tv
                def mkw(rt=rt, g0=g0, gn=gn, nr=nr, gbase=gbase, grs=grs):
                    gofs = gbase + g0 * 128 * grs + rt * 128
                    return lambda E: E.dma_start(
                        out=bass.AP(wgin, gofs,
                                    [[grs, 128], [128 * grs, gn], [1, nr]]),
                        in_=tev[:, 0:gn * 128].rearrange(
                            "p (g j) -> p g j", g=gn)[:, :, 0:nr])
                t_wgin_writes.append(p.op("sync", mkw(), deps=[tv],
                                               dma="dgw"))

    def _wcc(E):
        ins_ = E.collective_compute(
            "AllGather", AL.bypass, replica_groups=[list(range(8))],
            ins=[wgin.ap().rearrange("(a b) -> a b", b=2048).opt()],
            outs=[wgall.ap().rearrange("(a b) -> a b", b=2048).opt()])
        ins_.then_inc(sems["cc0"], 1)
        return ins_
    p.op("gpsimd", _wcc, deps=list(t_wgin_writes), inc=False)
    t_wcc = ("cc0", 0, 1, True)
    if dbg:
        t_wcc = p.op("sync", lambda E: E.dma_start(out=dbg_wg.ap(),
                                                   in_=wgall.ap()),
                     deps=[t_wcc], dma="dgw")


    def finish_truncated():
        bars = [p.barrier_tag(en) for en in ENGS]
        p.op("sync", lambda E: E.nop(), deps=bars, inc=False)
        p.emit(block, sems)
        ctx.close()
        return nc

    if stop_after == "W":
        return finish_truncated()

    t_wkv = []
    for c in range(8):
        def mk(c=c):
            return lambda E: E.dma_start(
                out=wbig[:, c * 2048:(c + 1) * 2048],
                in_=wqkvt[c * 128:(c + 1) * 128, D:3 * D])
        t_wkv.append(p.op("gpsimd", mk(), dma="dgp"))

    t_cadd1 = None
    t_czero = p.op("vector", lambda E: E.memset(uq[:, 0:1536], 0.0))
    t_vones1 = p.op("vector", lambda E: E.memset(
        vs1.rearrange("p (h c) -> p h c", c=65)[:, :, 64:65], 1.0))
    k_store = [None] * NSUB
    v_store = [None] * NSUB
    xnfm_store = [None] * NSUB
    xqfm_store = [None] * NSUB
    prev_kevac = [None, None]
    prev_vevac = [None, None]
    prev_q_tag = [prev_tq, prev_tq]
    xs_war = [prev_tq, prev_tq]
    pdbf_reader = [w_last_tv, None]
    xq_stage_reader = [None, None]
    last_mm_t2 = [None, None]
    half_evac = [None, None]
    par_bufs = [
        dict(xs=xs, xn=xn, xnb=xnb, xqb=xqb, tevp=tev[:, 0:1024],
             ks=ks[:, :], vs=vs[:, :], s1=st1[:, :], s2=st2[:, :],
             s3=st3[:, :], s4=st4[:, :], qa=qslc[0], qb=qslc[1],
             qc=qslc[2], bm=qbm[:, 0:64], sc=qsc[:, 0:64], rc=qrc[:, 0:64]),
        dict(xs=xs1, xn=xn1, xnb=xnb1, xqb=xqb1, tevp=tev1b, ks=ks1,
             vs=vs1, s1=st1b[:, :], s2=st2b[:, :], s3=st3b[:, :],
             s4=st4b[:, :], qa=qA1, qb=qB1, qc=qC1, bm=qbm2[:, 0:64],
             sc=qsc2[:, 0:64], rc=qrc2[:, 0:64]),
    ]

    for ip in range(NSUB // 2):
        ii = [2 * ip, 2 * ip + 1]
        BB = [par_bufs[0], par_bufs[1]]
        txs = []
        for z in range(2):
            txs.append(p.op("sync", lambda E, i=ii[z], B=BB[z]: E.dma_start(
                out=B["xs"], in_=x_in[i * 128:(i + 1) * 128, :]),
                deps=[xs_war[z]], dma="dck"))
        tld = p.stream_tag("dck")
        ts1_, ts2_, tmu_, tv2_, tm2a_, tm2_, tsq_, trs_, txn_, txnb_ = \
            [[None, None] for _ in range(10)]
        for z in range(2):
            ts1_[z] = p.op("vector", lambda E, B=BB[z]: E.reduce_sum(
                B["s1"], B["xs"], axis=AX), deps=[tld])
        for z in range(2):
            ts2_[z] = p.op("scalar", lambda E, B=BB[z]: E.activation(
                B["qa"], B["xs"], AF.Square, accum_out=B["s2"]),
                deps=[tld, prev_q_tag[z]])
        for z in range(2):
            tmu_[z] = p.op("vector", lambda E, B=BB[z]: E.tensor_scalar(
                B["s1"], B["s1"], 1.0 / D, None, AL.mult), deps=[ts1_[z]])
        for z in range(2):
            tv2_[z] = p.op("vector", lambda E, B=BB[z]: E.tensor_scalar(
                B["s2"], B["s2"], 1.0 / D, None, AL.mult), deps=[ts2_[z]])
        for z in range(2):
            tm2a_[z] = p.op("vector", lambda E, B=BB[z]: E.tensor_scalar(
                B["s4"], B["s1"], B["s1"][:, 0:1], None, AL.mult),
                deps=[tmu_[z]])
        for z in range(2):
            tm2_[z] = p.op("vector", lambda E, B=BB[z]: E.tensor_tensor(
                B["s3"], B["s2"], B["s4"], AL.subtract),
                deps=[tm2a_[z], tv2_[z]])
        for z in range(2):
            tsq_[z] = p.op("scalar", lambda E, B=BB[z]: E.activation(
                B["s4"], B["s3"], AF.Ln, bias=epsv[:, 0:1], scale=1.0),
                deps=[tm2_[z], t_eps])
        for z in range(2):
            trs_[z] = p.op("scalar", lambda E, B=BB[z]: E.activation(
                B["s3"], B["s4"], AF.Exp, scale=-0.5), deps=[tsq_[z]])
        for z in range(2):
            txn_[z] = p.op("vector", lambda E, B=BB[z]: E.tensor_scalar(
                B["xn"], B["xs"], B["s1"][:, 0:1], B["s3"][:, 0:1],
                AL.subtract, AL.mult), deps=[trs_[z], tmu_[z]])
        for z in range(2):
            txnb_[z] = p.op("vector", lambda E, B=BB[z]: E.tensor_scalar(
                B["xnb"], B["xs"], B["s1"][:, 0:1], B["s3"][:, 0:1],
                AL.subtract, AL.mult),
                deps=[trs_[z], tmu_[z]] + ([prev_last_mm] if ip == 0 else []))
        tev1_ = [None, None]
        for z in range(2):
            mm_t = []
            for c in range(8):
                def mkt(c=c, z=z, B=BB[z]):
                    return lambda E: E.transpose(
                        pDbf[:, z * 1024 + c * 128: z * 1024 + (c + 1) * 128],
                        B["xnb"][:, c * 128:(c + 1) * 128], ident[:, :])
                dd = []
                if c == 0:
                    dd = [txnb_[z], t_ident, pdbf_reader[z], t_pzD]
                mm_t.append(p.op("tensor", mkt(), deps=dd))
            tev1_[z] = p.op("scalar", lambda E, z=z, B=BB[z]: E.activation(
                B["tevp"], pDbf[:, z * 1024:z * 1024 + 1024], AF.Copy),
                deps=mm_t + (t_wgin_writes[-1:] if ip == 0 else
                             [xqfm_store[ii[1] - 2]]))
            pdbf_reader[z] = tev1_[z]
        for z in range(2):
            t_st1 = p.op("sync", lambda E, i=ii[z], B=BB[z]: E.dma_start(
                out=xnfm_d.rearrange("(c p) t -> p c t", p=128)[:, :,
                                     i * 128:(i + 1) * 128],
                in_=B["tevp"].rearrange("p (c j) -> p c j", c=8)),
                deps=[tev1_[z]], dma="dst_xn")
            xnfm_store[ii[z]] = t_st1
        kv_evac = [[None] * 4, [None] * 4]
        for blkz, gl, gh in [(0, 0, 1), (1, 0, 1), (0, 2, 3), (1, 2, 3)]:
            B = BB[blkz]
            for g in (gl, gh):
                half = g % 2
                mm_tags = []
                for c in range(8):
                    def mkm(g=g, c=c, half=half, B=B):
                        return lambda E: E.matmul(
                            pA[:, half * 512:(half + 1) * 512],
                            B["tevp"][:, c * 128:(c + 1) * 128],
                            wbig[:, c * 2048 + g * 512:
                                 c * 2048 + (g + 1) * 512],
                            start=(c == 0), stop=(c == 7))
                    dd = []
                    if c == 0:
                        dd = [tev1_[blkz], t_pzA] + (
                            t_wkv if (g == 0 and ip == 0 and blkz == 0)
                            else [])
                        if half_evac[half] is not None:
                            dd.append(half_evac[half])
                    mm_tags.append(p.op("tensor", mkm(), deps=dd))
                if g < 2:
                    tk = p.op("scalar",
                              lambda E, g=g, half=half, B=B: E.activation(
                                  B["ks"][:, g * 512:(g + 1) * 512],
                                  pA[:, half * 512:(half + 1) * 512],
                                  AF.Exp),
                              deps=[mm_tags[-1]] +
                              ([prev_kevac[blkz]] if (g == 0 and
                                                      prev_kevac[blkz])
                               else []))
                else:
                    def mkv(g=g, half=half, B=B):
                        return lambda E: E.activation(
                            B["vs"].rearrange("p (h c) -> p h c", c=65)[
                                :, (g - 2) * 8:(g - 2) * 8 + 8, 0:64],
                            pA[:, half * 512:(half + 1) * 512].rearrange(
                                "p (h e) -> p h e", e=64), AF.Copy)
                    tk = p.op("scalar", mkv(),
                              deps=[mm_tags[-1]] +
                              ([prev_vevac[blkz]] if (g == 2 and
                                                      prev_vevac[blkz])
                               else []))
                kv_evac[blkz][g] = tk
                half_evac[half] = tk
        for z in range(2):
            tks = p.op("sync", lambda E, i=ii[z], B=BB[z]: E.dma_start(
                out=k_d[i * 128:(i + 1) * 128, :], in_=B["ks"]),
                deps=[kv_evac[z][0], kv_evac[z][1]], dma="dst_k")
            tvs = p.op("sync", lambda E, i=ii[z], B=BB[z]: E.dma_start(
                out=v_d[i * 128:(i + 1) * 128, :], in_=B["vs"]),
                deps=[kv_evac[z][2], kv_evac[z][3], t_vones, t_vones1],
                dma="dst_v")
            k_store[ii[z]] = tks
            v_store[ii[z]] = tvs
            prev_kevac[z] = tks
            prev_vevac[z] = tvs
        prev_kevac[0] = prev_kevac[1]
        prev_vevac[0] = prev_vevac[1]
        cd_all = []
        for z in range(2):
            for pr in range(NPAIR):
                def mkc(pr=pr, z=z, B=BB[z]):
                    return lambda E: E.matmul(
                        pC[:, pair_off(pr):pair_off(pr) + 130],
                        B["ks"][:, pr * 128:(pr + 1) * 128],
                        B["vs"][:, pr * 130:(pr + 1) * 130],
                        start=(z == 0 and pr % 3 == 0),
                        stop=(z == 1 and pr in (2, 5, 7)))
                dd = []
                if z == 0 and pr == 0:
                    dd = (kv_evac[0] + kv_evac[1] +
                          [t_vones, t_vones1, t_pzC] +
                          ([t_cadd1] if t_cadd1 is not None else []))
                cd_all.append(p.op("tensor", mkc(), deps=dd))
        t_cadd1 = p.op("vector", lambda E: E.scalar_tensor_tensor(
            uq[:, 0:1536], pC.ap(), 1.0, uq[:, 0:1536], AL.mult, AL.add),
            deps=[cd_all[-1]] + ([t_cadd1] if t_cadd1 is not None else
                                 [t_czero]))
        jobs = []
        for z in range(2):
            B = BB[z]
            jobs.append(dict(x=B["xn"], q=B["xqb"], bm=B["bm"], sc=B["sc"],
                             rc=B["rc"], t1=B["qa"], t2=B["qb"], t3=B["qc"],
                             F=D, dep=[txn_[z], ts2_[z]],
                             final_war=([prev_last_mm] if ip == 0 else
                                        [last_mm_t2[z]] if last_mm_t2[z]
                                        else [])))
        tqs = emit_quant_multi(p, jobs, mskt[:, 0:1])
        for z in range(2):
            prev_q_tag[z] = tqs[z]
        for z in range(2):
            stage = (pDbf, 2048) if z == 0 else (pAbf, 0)
            mm_t2 = []
            for c in range(8):
                def mkt2(c=c, z=z, B=BB[z], stage=stage):
                    return lambda E: E.transpose(
                        stage[0][:, stage[1] + c * 128:
                                 stage[1] + (c + 1) * 128],
                        B["xqb"][:, c * 128:(c + 1) * 128], ident[:, :])
                dd = []
                if c == 0:
                    dd = [tqs[z]]
                    if xq_stage_reader[z] is not None:
                        dd.append(xq_stage_reader[z])
                    if z == 1:
                        dd += [kv_evac[0][2], kv_evac[0][3],
                               kv_evac[1][2], kv_evac[1][3]]
                mm_t2.append(p.op("tensor", mkt2(), deps=dd))
            last_mm_t2[z] = mm_t2[-1]
            tev2 = p.op("scalar", lambda E, z=z, B=BB[z], stage=stage:
                        E.activation(B["tevp"],
                                     stage[0][:, stage[1]:stage[1] + 1024],
                                     AF.Copy),
                        deps=mm_t2 + [xnfm_store[ii[1]]])
            xq_stage_reader[z] = tev2
            if z == 1:
                half_evac[0] = tev2
                half_evac[1] = tev2
            t_st2 = p.op("sync", lambda E, i=ii[z], B=BB[z]: E.dma_start(
                out=xqfm_d.rearrange("(c p) t -> p c t", p=128)[:, :,
                                     i * 128:(i + 1) * 128],
                in_=B["tevp"].rearrange("p (c j) -> p c j", c=8)),
                deps=[tev2], dma="dst_xq")
            xqfm_store[ii[z]] = t_st2
            xs_war[z] = txnb_[z]

    if stop_after == "1":
        return finish_truncated()

    t_cst = p.op("sync", lambda E: E.dma_start(out=cgin.ap(),
                                               in_=uq[:, 0:1536]),
                 deps=[t_cadd1], dma="dcg")
    t_cev = t_cadd1
    def _ccc(E):
        ins_ = E.collective_compute(
            "AllGather", AL.bypass, replica_groups=[list(range(8))],
            ins=[cgin.ap().opt()], outs=[cgall.ap().opt()])
        ins_.then_inc(sems["cc1"], 1)
        return ins_
    p.op("gpsimd", _ccc, deps=[t_cst], inc=False)
    t_ccc = ("cc1", 0, 1, True)
    if dbg:
        t_ccc = p.op("sync", lambda E: E.dma_start(out=dbg_cg.ap(),
                                                   in_=cgall.ap()),
                     deps=[t_ccc], dma="dcg")

    def load_stripe(si, deps):
        s0, W = STRIPES[si]
        tags = []
        for dst, gb in [(wgs, 0), (wus, SHE)]:
            for r in range(8):
                lo = max(s0, r * WSH)
                hi = min(s0 + W, (r + 1) * WSH)
                if lo >= hi:
                    continue
                def mkl(r=r, lo=lo, hi=hi, dst=dst, gb=gb, s0=s0, W=W):
                    gofs = r * GINE + gb + (lo - r * WSH)
                    return lambda E: E.dma_start(
                        out=dst[:, :].rearrange("p (c w) -> p c w", w=768)[
                            :, :, lo - s0:hi - s0],
                        in_=bass.AP(wgall, gofs,
                                    [[WSH, 128], [128 * WSH, 8],
                                     [1, hi - lo]]))
                tags.append(p.op("sync", mkl(), deps=deps, dma="dstrip"))
        ncc = W // 128
        for cc in range(ncc):
            def mkd(cc=cc, s0=s0):
                gofs = 2 * SHE + (s0 + cc * 128) * DSH
                return lambda E: E.dma_start(
                    out=wds[:, cc * 1024:(cc + 1) * 1024].rearrange(
                        "p (r j) -> p r j", r=8),
                    in_=bass.AP(wgall, gofs,
                                [[DSH, 128], [GINE, 8], [1, DSH]]))
            tags.append(p.op("sync", mkd(), deps=deps, dma="dstrip"))
        return tags

    tgs_p = t_cadd1
    th_p = xq_stage_reader[0]
    b_pa_war = [half_evac[0], half_evac[1], xq_stage_reader[1]]
    b_carve_bar = p.barrier_tag("tensor")
    t_ws = load_stripe(0, [t_wcc, b_carve_bar,
                           p.barrier_tag("vector"), p.barrier_tag("gpsimd"),
                           p.stream_tag("dst_xn"), p.stream_tag("dst_xq"),
                           p.stream_tag("dst_k"), p.stream_tag("dst_v")] +
                       b_pa_war)
    for i in range(NSUB):
        def mka(i=i):
            return lambda E: E.dma_start(
                out=acc[:, i * D:(i + 1) * D],
                in_=x_in[i * 128:(i + 1) * 128, :])
        p.op("sync", mka(), dma="dacc")
    t_accs = p.stream_tag("dacc")
    t_wq_q = []
    wq_bar = p.barrier_tag("tensor")
    for c in range(8):
        def mkq(c=c):
            return lambda E: E.dma_start(
                out=wbig[:, c * 1024:(c + 1) * 1024],
                in_=wqkvt[c * 128:(c + 1) * 128, 0:D])
        t_wq_q.append(p.op("gpsimd", mkq(), dma="dgp",
                           deps=[wq_bar] if c == 0 else []))
    prev_hfm = [None, None]
    prev_dn_ev = None
    t_acc_fin = {j: t_accs for j in range(NSUB)}
    last_gu_mm = [None, None]
    fb = [
        dict(xqt=xqt[:, :], gsil=gsil, hbuf=hbuf, hqb=hqb, hfm=hfm[:, :],
             bm=qbm[:, 0:48], sc=qsc[:, 0:48], rc=qrc[:, 0:48], f=fslc,
             stg=(pDbf, 2048)),
        dict(xqt=xqt1, gsil=gsil1, hbuf=hbuf1, hqb=hqb1, hfm=hfm1,
             bm=qbm3, sc=qsc3, rc=qrc3, f=f1,
             stg=(pDbf, 2048)),
    ]
    prev_rb = [None, None]
    prev_th = [None, None]

    for si in range(4):
        s0, W = STRIPES[si]
        ncc = W // 128
        ngroups = [(0, 512), (512, W - 512)] if W > 512 else [(0, W)]

        def emit_gu(j, z, gdeps, udeps):
            nonlocal last_gu_mm
            F = fb[z]
            txq = p.op("sync", lambda E, j=j, F=F: E.dma_start(
                out=F["xqt"].rearrange("p (c t) -> p c t", c=8),
                in_=xqfm_d.rearrange("(c p) t -> p c t", p=128)[:, :,
                                     j * 128:(j + 1) * 128]),
                deps=[p.stream_tag("dst_xq")] +
                     ([last_gu_mm[z]] if last_gu_mm[z] else
                      [b_carve_bar] + b_pa_war),
                dma="dxqt" if z == 0 else "dxqt1")
            gmm, umm = [], []
            for (o0, nw) in ngroups:
                for c in range(8):
                    def mkg(c=c, o0=o0, nw=nw, W=W, F=F):
                        return lambda E: E.matmul(
                            pC[:, o0:o0 + nw],
                            F["xqt"][:, c * 128:(c + 1) * 128],
                            wgs[:, c * 768 + o0: c * 768 + o0 + nw],
                            start=(c == 0), stop=(c == 7))
                    dd = ([txq] + t_ws + gdeps) if (c == 0 and o0 == 0) else []
                    gmm.append(p.op("tensor", mkg(), deps=dd))
            for (o0, nw) in ngroups:
                for c in range(8):
                    def mku(c=c, o0=o0, nw=nw, W=W, F=F):
                        return lambda E: E.matmul(
                            pD[:, o0:o0 + nw],
                            F["xqt"][:, c * 128:(c + 1) * 128],
                            wus[:, c * 768 + o0: c * 768 + o0 + nw],
                            start=(c == 0), stop=(c == 7))
                    dd = udeps if (c == 0 and o0 == 0) else []
                    umm.append(p.op("tensor", mku(), deps=dd))
            last_gu_mm[z] = umm[-1]
            return gmm, umm

        guq = {}
        guq[0] = emit_gu(0, 0, [tgs_p],
                         [pdbf_reader[0], pdbf_reader[1]]
                         if si == 0 else ([th_p] if th_p else []))
        for jp in range(NSUB // 2):
            j0, j1 = 2 * jp, 2 * jp + 1
            gmm0, umm0 = guq[j0]
            tgs0 = p.op("scalar", lambda E, W=W, F=fb[0]: E.activation(
                F["gsil"][:, 0:W], pC.ap()[:, 0:W], AF.Silu),
                deps=[gmm0[-1]] + ([prev_th[0]] if prev_th[0] else []))
            th0 = p.op("vector", lambda E, W=W, F=fb[0]: E.tensor_tensor(
                F["hbuf"][:, 0:W], pD.ap()[:, 0:W], F["gsil"][:, 0:W],
                AL.mult),
                deps=[umm0[-1], tgs0] + ([prev_rb[0]] if prev_rb[0] else []))
            guq[j1] = emit_gu(j1, 1, [tgs0], [th0])
            gmm1, umm1 = guq[j1]
            tgs1 = p.op("scalar", lambda E, W=W, F=fb[1]: E.activation(
                F["gsil"][:, 0:W], pC.ap()[:, 0:W], AF.Silu),
                deps=[gmm1[-1]] + ([prev_th[1]] if prev_th[1] else []))
            th1 = p.op("vector", lambda E, W=W, F=fb[1]: E.tensor_tensor(
                F["hbuf"][:, 0:W], pD.ap()[:, 0:W], F["gsil"][:, 0:W],
                AL.mult),
                deps=[umm1[-1], tgs1] + ([prev_rb[1]] if prev_rb[1] else []))
            jobs = []
            for z, thz in ((0, th0), (1, th1)):
                F = fb[z]
                jobs.append(dict(x=F["hbuf"][:, 0:W], q=F["hqb"][:, 0:W],
                                 bm=F["bm"][:, 0:W // 16],
                                 sc=F["sc"][:, 0:W // 16],
                                 rc=F["rc"][:, 0:W // 16],
                                 t1=F["f"][0][:, 0:W], t2=F["f"][1][:, 0:W],
                                 t3=F["f"][2][:, 0:W], F=W, dep=[thz],
                                 final_war=[prev_hfm[z]] if prev_hfm[z]
                                 else []))
            tqhs = emit_quant_multi(p, jobs, mskt[:, 0:1])
            prev_rb = list(p.q_rb_tags)
            if jp + 1 < NSUB // 2:
                guq[2 * jp + 2] = emit_gu(2 * jp + 2, 0,
                                          [tgs1, prev_hfm[1] or tqhs[1]],
                                          [th1])
            thfs = [None, None]
            for z, (tqh, jz) in enumerate(((tqhs[0], j0), (tqhs[1], j1))):
                F = fb[z]
                stgap, stgo = F["stg"]
                mm_t = []
                for cc in range(ncc):
                    def mkt(cc=cc, F=F, stgap=stgap, stgo=stgo):
                        return lambda E: E.transpose(
                            stgap[:, stgo + cc * 128: stgo + (cc + 1) * 128],
                            F["hqb"][:, cc * 128:(cc + 1) * 128], ident[:, :])
                    dd = []
                    if cc == 0:
                        dd = [tqh]
                        for t_ in (prev_hfm[0], prev_hfm[1]):
                            if t_ is not None:
                                dd.append(t_)
                        if th_p is not None:
                            dd.append(th_p)
                            th_p = None
                        if z == 1 and thfs[0] is not None:
                            dd.append(thfs[0])
                    mm_t.append(p.op("tensor", mkt(), deps=dd))
                thf = p.op("scalar", lambda E, ncc=ncc, F=F, stgap=stgap,
                           stgo=stgo: E.activation(
                               F["hfm"][:, 0:ncc * 128],
                               stgap[:, stgo:stgo + ncc * 128], AF.Copy),
                           deps=mm_t)
                prev_hfm[z] = thf
                thfs[z] = thf
            for z, jz in ((0, j0), (1, j1)):
                F = fb[z]
                dmm = []
                for ng in range(2):
                    for cc in range(ncc):
                        def mkdn(cc=cc, ng=ng, ncc=ncc, F=F):
                            return lambda E: E.matmul(
                                pA[:, ng * 512:(ng + 1) * 512],
                                F["hfm"][:, cc * 128:(cc + 1) * 128],
                                wds[:, cc * 1024 + ng * 512:
                                     cc * 1024 + (ng + 1) * 512],
                                start=(cc == 0), stop=(cc == ncc - 1))
                        dd = []
                        if cc == 0 and ng == 0:
                            dd = [thfs[z]]
                            if prev_dn_ev is not None:
                                dd.append(prev_dn_ev)
                            else:
                                dd += b_pa_war
                        dmm.append(p.op("tensor", mkdn(), deps=dd))
                tdn = p.op("vector", lambda E, j=jz: E.scalar_tensor_tensor(
                    acc[:, j * D:(j + 1) * D], pA.ap(), 1.0,
                    acc[:, j * D:(j + 1) * D], AL.mult, AL.add),
                    deps=[dmm[-1], t_acc_fin[jz]])
                prev_dn_ev = tdn
                t_acc_fin[jz] = tdn
            prev_th = [th0, th1]
        tgs_p = tgs1
        th_p = th1
        if si + 1 < 4:
            t_ws = load_stripe(si + 1, [p.barrier_tag("tensor")])
            guq[0] = emit_gu(0, 0, [tgs_p, prev_hfm[1] or tgs_p],
                             [th_p] if th_p else [])

    b_last_tgs = tgs_p
    b_last_th = th_p
    b_quant_done = [prev_hfm[0], prev_hfm[1], prev_dn_ev]

    t_cini = p.op("vector", lambda E: E.memset(cstA[:, :], 0.0))
    t_off = t_cini
    for r in range(8):
        def mkld(r=r):
            return lambda E: E.dma_start(
                out=cstB[:, :], in_=cgall.ap()[r * 128:(r + 1) * 128, :])
        tl = p.op("sync", mkld(), deps=[t_ccc, t_off, t_cst], dma="dcg")
        def mkacc(r=r):
            return lambda E: E.scalar_tensor_tensor(
                cstA[:, :], cstB[:, :], mko[:, r:r + 1], cstA[:, :],
                AL.mult, AL.add)
        t_off = p.op("vector", mkacc(), deps=[tl, t_mask, t_off])

    if stop_after == "C":
        if dbg:
            p.op("sync", lambda E: E.dma_start(out=dbg_cst.ap(),
                                               in_=cstA[:, :]),
                 deps=[t_off], dma="dcg")
            p.op("sync", lambda E: E.dma_start(out=dbg_mk.ap(),
                                               in_=mko[:, :]),
                 deps=[t_mask], dma="dcg")
        return finish_truncated()

    tens_bar = p.barrier_tag("tensor")
    prev_outp = prev_dn_ev
    t_prev_cadd = b_last_tgs
    t_prev_attn_evac = [b_last_th, prev_hfm[0], prev_hfm[1]]
    t_wo = []
    for c in range(8):
        def mk(c=c):
            return lambda E: E.dma_start(
                out=wbig[:, 8192 + c * 1024: 8192 + (c + 1) * 1024],
                in_=woutt[c * 128:(c + 1) * 128, :])
        t_wo.append(p.op("gpsimd", mk(), dma="dgo",
                         deps=([tens_bar] + b_quant_done) if c == 0
                         else []))

    cst = [cstA, cstB]
    t_cprev = t_off
    acc_outp = {}
    prev_u = None
    prev_afm_ev = None
    t_acc_load = {}
    prev_kq = None
    prev_vq = None

    for q in range(NQT):
        i0 = q * 2
        tqx = p.op("sync", lambda E, q=q: E.dma_start(
            out=xnq[:, :].rearrange("p (c t) -> p c t", c=8),
            in_=xnfm_d.rearrange("(c p) t -> p c t", p=128)[:, :,
                                 q * QTT:(q + 1) * QTT]),
            deps=[p.stream_tag("dst_xn")] +
                 ([prev_u] if prev_u else []), dma="dqx")
        t_uex = [None] * 8
        for m in range(8):
            half = m % 2
            mm_tags = []
            for c in range(8):
                def mkm(m=m, c=c, half=half):
                    return lambda E: E.matmul(
                        pA[:, half * 512: half * 512 + QTT],
                        wbig[:, c * 1024 + m * 128: c * 1024 + (m + 1) * 128],
                        xnq[:, c * QTT:(c + 1) * QTT],
                        start=(c == 0), stop=(c == 7))
                dd = []
                if c == 0:
                    dd = [tqx]
                    if m == 0 and q == 0:
                        dd += t_wq_q
                        dd.append(half_evac[0])
                        dd.append(half_evac[1])
                    if m >= 2:
                        dd.append(t_uex[m - 2])
                    if m < 2 and prev_outp is not None:
                        dd.append(prev_outp)
                mm_tags.append(p.op("tensor", mkm(), deps=dd))
            t_uex[m] = p.op("scalar", lambda E, m=m, half=half: E.activation(
                uq[:, m * QTT:(m + 1) * QTT],
                pA[:, half * 512: half * 512 + QTT], AF.Exp),
                deps=[mm_tags[-1]] + ([t_cst] if q == 0 else []))
        prev_u = t_uex[-1]
        if stop_after == "A0":
            continue
        for s in range(2):
            i = i0 + s
            tkq = p.op("sync", lambda E, i=i: E.dma_start(
                out=kq[:, :], in_=k_d[i * 128:(i + 1) * 128, :]),
                deps=[p.stream_tag("dst_k")] +
                     ([prev_kq] if prev_kq else []),
                dma="dkq")
            tvq = p.op("sync", lambda E, i=i: E.dma_start(
                out=vq[:, :], in_=v_d[i * 128:(i + 1) * 128, :]),
                deps=[p.stream_tag("dst_v")] +
                     ([prev_vq] if prev_vq else []),
                dma="dvq")
            for b in range(2):
                n = s * 2 + b
                g = q * 4 + n
                cur = cst[g % 2]
                nxt = cst[(g + 1) % 2]
                attn_mm = []
                horder = [hh for hh in range(NH) if hh % 2 == 0] + \
                         [hh for hh in range(NH) if hh % 2 == 1]
                for h in horder:
                    pr = h // 2
                    par = h % 2
                    def mka_(h=h, pr=pr, par=par, n=n, cur=cur):
                        return lambda E: E.matmul(
                            pD[0:64, head_off(h):head_off(h) + 65],
                            uq[par * 64:par * 64 + 64,
                               pr * QTT + n * 64: pr * QTT + (n + 1) * 64],
                            cur[par * 64:par * 64 + 64,
                                pair_off(pr) + par * 65:
                                pair_off(pr) + par * 65 + 65],
                            start=True, stop=True)
                    dd = []
                    if h == 0:
                        dd = [t_cprev, t_uex[7]] + t_prev_attn_evac
                    attn_mm.append(p.op("tensor", mka_(), deps=dd))
                cd_mm = []
                for pr in range(NPAIR):
                    def mkc_(pr=pr, b=b):
                        return lambda E: E.matmul(
                            pC[:, pair_off(pr):pair_off(pr) + 130],
                            kq[b * 64:(b + 1) * 64,
                               pr * 128:(pr + 1) * 128],
                            vq[b * 64:(b + 1) * 64,
                               pr * 130:(pr + 1) * 130],
                            start=True, stop=True)
                    dd = []
                    if pr == 0:
                        dd = [tkq, tvq, t_prev_cadd]
                    cd_mm.append(p.op("tensor", mkc_(), deps=dd))
                t_cprev = p.op(
                    "vector",
                    lambda E, cur=cur, nxt=nxt: E.scalar_tensor_tensor(
                        nxt[:, :], pC.ap(), 1.0, cur[:, :], AL.mult, AL.add),
                    deps=[cd_mm[-1], t_cprev])
                t_prev_cadd = t_cprev
                tden = []
                for gi, (h0, hn) in enumerate(HEAD_GROUPS):
                    def mkd_(gi=gi, h0=h0, hn=hn):
                        base = 512 * gi
                        return lambda E: E.tensor_scalar(
                            dn[:, h0:h0 + hn],
                            pD.ap()[0:64, base:base + 65 * hn].rearrange(
                                "p (h c) -> p h c", c=65)[:, :,
                                                          64:65].squeeze(2),
                            1e-30, None, AL.max)
                    tden.append(p.op("vector", mkd_(), deps=[attn_mm[-1]]))
                trec = p.op("vector", lambda E: E.reciprocal(dq[:, :],
                                                             dn[:, :]),
                            deps=tden)
                ev_tags = []
                for gi, (h0, hn) in enumerate(HEAD_GROUPS):
                    def mke_(gi=gi, h0=h0, hn=hn, b=b):
                        base = 512 * gi
                        return lambda E: E.tensor_tensor(
                            atm[b * 64:(b + 1) * 64,
                                h0 * 64:(h0 + hn) * 64].rearrange(
                                    "p (h e) -> p h e", e=64),
                            pD.ap()[0:64, base:base + 65 * hn].rearrange(
                                "p (h c) -> p h c", c=65)[:, :, 0:64],
                            dq[:, h0:h0 + hn].unsqueeze(2).broadcast_to(
                                [64, hn, 64]), AL.mult)
                    ev_tags.append(p.op("vector", mke_(), deps=[trec]))
                t_prev_attn_evac = ev_tags
            prev_kq = t_prev_cadd
            prev_vq = t_prev_cadd
            if stop_after == "A1":
                continue
            mm_t = []
            for c in range(8):
                def mkt_(c=c):
                    return lambda E: E.transpose(
                        pAbf[:, c * 128:(c + 1) * 128],
                        atm[:, c * 128:(c + 1) * 128],
                        ident[:, :])
                dd = []
                if c == 0:
                    dd = list(t_prev_attn_evac)
                    if prev_afm_ev is not None:
                        dd.append(prev_afm_ev)
                    if prev_outp is not None:
                        dd.append(prev_outp)
                mm_t.append(p.op("tensor", mkt_(), deps=dd))
            tafm = p.op("vector", lambda E: E.tensor_copy(afm[:, :],
                                                          pAbf[:, 0:1024]),
                        deps=mm_t)
            prev_afm_ev = tafm
            mm_o = []
            for ng in range(2):
                for c in range(8):
                    def mko_(c=c, ng=ng):
                        return lambda E: E.matmul(
                            pA[:, ng * 512:(ng + 1) * 512],
                            afm[:, c * 128:(c + 1) * 128],
                            wbig[:, 8192 + c * 1024 + ng * 512:
                                 8192 + c * 1024 + (ng + 1) * 512],
                            start=(c == 0), stop=(c == 7))
                    dd = ([tafm, t_wo[-1]] if (c == 0 and ng == 0) else [])
                    mm_o.append(p.op("tensor", mko_(), deps=dd))
            t_out = p.op("vector", lambda E, i=i: E.scalar_tensor_tensor(
                acc[:, i * D:(i + 1) * D], pA.ap(), 1.0,
                acc[:, i * D:(i + 1) * D], AL.mult, AL.add),
                deps=[mm_o[-1], t_acc_fin[i]])
            acc_outp[i] = t_out
            prev_outp = t_out

    if stop_after in ("A0", "A1"):
        return finish_truncated()
    if stop_after == "A":
        sta = []
        for i in range(NSUB):
            def mksa(i=i):
                return lambda E: E.dma_start(
                    out=out[i * 128:(i + 1) * 128, :],
                    in_=acc[:, i * D:(i + 1) * D])
            sta.append(p.op("sync", mksa(), deps=[acc_outp[i]],
                            dma="dout"))
        p.op("sync", lambda E: E.nop(), deps=sta, inc=False)
        return finish_truncated()

    st_tags = []
    for i in range(NSUB):
        def mks(i=i):
            return lambda E: E.dma_start(out=out[i * 128:(i + 1) * 128, :],
                                         in_=acc[:, i * D:(i + 1) * D])
        st_tags.append(p.op("sync", mks(), deps=[acc_outp[i]],
                                dma="dout"))
    p.op("sync", lambda E: E.nop(), deps=st_tags, inc=False)

    p.emit(block, sems)
    ctx.close()
    return nc


_NC_CACHE = None


def kernel(L0_attn_x, L0_attn_W_qkv, L0_attn_W_out, L0_ffn_w_gate,
           L0_ffn_w_up, L0_ffn_w_down):
    global _NC_CACHE
    x = np.asarray(L0_attn_x, dtype=np.float32)
    wqkv = np.asarray(L0_attn_W_qkv, dtype=np.float32)
    wout = np.asarray(L0_attn_W_out, dtype=np.float32)
    wg = np.asarray(L0_ffn_w_gate, dtype=np.float32)
    wu = np.asarray(L0_ffn_w_up, dtype=np.float32)
    wd = np.asarray(L0_ffn_w_down, dtype=np.float32)

    wqkvt = np.ascontiguousarray(wqkv.T)
    woutt = np.ascontiguousarray(wout.T)

    in_maps = []
    for c in range(8):
        b, t0 = c // 4, (c % 4) * TC
        m = np.zeros(8, dtype=np.float32)
        m[(c // 4) * 4:c] = 1.0
        in_maps.append({
            "x_c": np.ascontiguousarray(x[b, t0:t0 + TC, :]),
            "wqkvt": wqkvt,
            "woutt": woutt,
            "wg_sh": np.ascontiguousarray(wg[c * WSH:(c + 1) * WSH, :]),
            "wu_sh": np.ascontiguousarray(wu[c * WSH:(c + 1) * WSH, :]),
            "wd_sh": np.ascontiguousarray(wd[c * DSH:(c + 1) * DSH, :]),
            "cmask": m,
            "crank": np.array([[c]], dtype=np.int32),
        })

    if _NC_CACHE is None:
        _NC_CACHE = build_core()
    res = run_bass_kernel_spmd(_NC_CACHE, in_maps, core_ids=list(range(8)))

    outp = np.empty((2, 8192, D), dtype=np.float32)
    for c in range(8):
        b, t0 = c // 4, (c % 4) * TC
        outp[b, t0:t0 + TC, :] = res.results[c]["out_c"]
    return outp
